# version 38
# baseline (speedup 1.0000x reference)
import numpy as np
import ml_dtypes
import concourse.bass as bass
import concourse.mybir as mybir
from concourse.bass_utils import run_bass_kernel_spmd

F32 = mybir.dt.float32
BF16 = mybir.dt.bfloat16
AF = mybir.ActivationFunctionType
ALU = mybir.AluOpType
AX = mybir.AxisListType

D = 1024
NH = 16
HD = 64
NIC = D // 128

RELAX_ENGINES = {"T", "S"}
DRAIN_NOW = False


_ALL_RES = []


class Res:
    __slots__ = ("last_w", "readers")

    def __init__(self):
        self.last_w = None
        self.readers = []
        _ALL_RES.append(self)


class Ctx:
    ENGINES = ("T", "V", "S", "D", "G")
    NDMA = 8

    def __init__(self):
        self.tokens = []
        self.counts = {e: 0 for e in ("T", "V", "S", "G")}
        self.dma_i = 0
        self.sems = {}
        self.mode = "dry"
        self.emit_eng = None
        self.proxy = None
        self.idx = 0
        self.watermark = {}

    def sem_names(self):
        return ["T", "V", "S", "G"] + [f"D{i}" for i in range(self.NDMA)]

    def begin(self, mode, eng=None, proxy=None):
        self.mode = mode
        self.emit_eng = eng
        self.proxy = proxy
        self.idx = 0
        self.dma_i = 0
        self.counts = {e: 0 for e in ("T", "V", "S", "G")}
        self.watermark = {s: 0 for s in self.sem_names()}
        for r in _ALL_RES:
            r.last_w = None
            r.readers = []

    def _wait(self, sem, val):
        if val <= 0 or self.watermark[sem] >= val:
            return
        self.watermark[sem] = val
        self.proxy.wait_ge(self.sems[sem], val)

    def emit(self, eng, fn, reads=(), writes=()):
        i = self.idx
        self.idx += 1
        if eng == "D":
            di = self.dma_i
            self.dma_i += 1
            pool = f"D{di % self.NDMA}"
            tok = (pool, 16 * (di // self.NDMA + 1))
        else:
            if self.mode == "dry":
                self.counts[eng] += 1
            tok = (eng, (self.counts[eng] if self.mode == "dry" else None))
        if self.mode == "dry":
            if eng == "D":
                self.tokens.append(tok)
            else:
                self.tokens.append(tok)
        else:
            tok = self.tokens[i]
            if eng == self.emit_eng:
                deps = []
                for r in reads:
                    if r.last_w is not None:
                        deps.append(r.last_w)
                for w in writes:
                    if w.last_w is not None:
                        deps.append(w.last_w)
                    deps.extend(w.readers)
                if eng == "D":
                    deps.append((tok[0], tok[1] - 16))
                for de, dv in deps:
                    if de == eng and dv >= tok[1]:
                        continue
                    if de == eng and eng in RELAX_ENGINES:
                        continue
                    self._wait(de, dv)
                instr = fn(self.proxy)
                instr.then_inc(self.sems[tok[0]], 16 if eng == "D" else 1)
        for r in reads:
            r.readers.append(tok)
        for w in writes:
            w.last_w = tok
            w.readers = []
        return tok

    def _snap(self):
        s = {e: self.counts[e] for e in ("T", "V", "S", "G")}
        for i in range(self.NDMA):
            n = self.dma_i
            s[f"D{i}"] = 16 * ((n - i + self.NDMA - 1) // self.NDMA)
        return s

    def barrier(self):
        i = self.idx
        self.idx += 1
        if self.mode == "dry":
            self.tokens.append(("BARRIER", self._snap()))
            return
        snap = self.tokens[i][1]
        for sname, v in snap.items():
            if sname != self.emit_eng:
                self._wait(sname, v)


def host_constants(T):
    ntt = T // 128
    freqs = (1.0 / (np.float32(10000.0) ** (np.arange(0, HD, 2, dtype=np.float32) / np.float32(HD)))).astype(np.float32)
    t = np.arange(T, dtype=np.float32)
    ang = np.outer(t, freqs).astype(np.float32)
    cos = np.cos(ang).astype(np.float32)
    sin = np.sin(ang).astype(np.float32)
    cos_s = np.ascontiguousarray(cos.reshape(ntt, 128, 32).transpose(1, 0, 2).reshape(128, ntt * 32))
    sin_s = np.ascontiguousarray(sin.reshape(ntt, 128, 32).transpose(1, 0, 2).reshape(128, ntt * 32))
    masks = np.zeros((128, 4 * 512), dtype=np.float32)
    kk = np.arange(128)[:, None]
    qq = np.arange(512)[None, :]
    for m in range(4):
        valid = (qq >= 128 * m + kk) | (qq >= 128 * (m + 1))
        dead = qq < 128 * m
        masks[:, m * 512:(m + 1) * 512] = np.where(dead, 0.0, np.where(valid, 1.0, 0.0))
    masks_bf = masks.astype(ml_dtypes.bfloat16)
    maskS = np.where(np.arange(128)[None, :] > np.arange(128)[:, None], -30000.0, 0.0).astype(np.float32)
    idn = np.eye(128, dtype=np.float32)
    idnb = idn.astype(ml_dtypes.bfloat16)
    return cos_s, sin_s, masks_bf, maskS, idn, idnb


def pack_biases(cn_b1, cn_b2, cn_b3, gc_b1, gc_b2):
    out = np.zeros((128, 20), dtype=np.float32)
    out[:, 0:8] = np.asarray(cn_b1).reshape(8, 128).T
    out[:, 8:12] = np.asarray(cn_b2).reshape(4, 128).T
    out[:, 12:14] = np.asarray(cn_b3).reshape(2, 128).T
    out[:, 14:18] = np.asarray(gc_b1).reshape(4, 128).T
    out[:, 18:20] = np.asarray(gc_b2).reshape(2, 128).T
    return out


def build_program(T):
    _ALL_RES.clear()
    ntt = T // 128
    nqc = T // 512
    nkt = T // 128

    nc = bass.Bass("TRN2", target_bir_lowering=False,
                   detect_race_conditions=(not RELAX_ENGINES))

    d_x = nc.dram_tensor("x", [T, D], F32, kind="ExternalInput")
    d_wqkv = nc.dram_tensor("wqkv", [3 * D, D], F32, kind="ExternalInput")
    d_wout = nc.dram_tensor("wout", [D, D], F32, kind="ExternalInput")
    d_cn1 = nc.dram_tensor("cn1", [D, D], F32, kind="ExternalInput")
    d_cn2 = nc.dram_tensor("cn2", [D, 512], F32, kind="ExternalInput")
    d_cn3 = nc.dram_tensor("cn3", [512, 256], F32, kind="ExternalInput")
    d_cn4 = nc.dram_tensor("cn4p", [128, 2], F32, kind="ExternalInput")
    d_gc1 = nc.dram_tensor("gc1", [D, 512], F32, kind="ExternalInput")
    d_gc2 = nc.dram_tensor("gc2", [512, 256], F32, kind="ExternalInput")
    d_gc3 = nc.dram_tensor("gc3p", [128, 2], F32, kind="ExternalInput")
    d_biasp = nc.dram_tensor("biasp", [128, 20], F32, kind="ExternalInput")
    d_scal3 = nc.dram_tensor("scal3", [1, 3], F32, kind="ExternalInput")
    d_cos = nc.dram_tensor("cosb", [128, ntt * 32], F32, kind="ExternalInput")
    d_sin = nc.dram_tensor("sinb", [128, ntt * 32], F32, kind="ExternalInput")
    d_mask = nc.dram_tensor("maskp", [128, 4 * 512], BF16, kind="ExternalInput")
    d_masksq = nc.dram_tensor("masksq", [128, 128], F32, kind="ExternalInput")
    d_idn = nc.dram_tensor("idn", [128, 128], F32, kind="ExternalInput")
    d_idnb = nc.dram_tensor("idnb", [128, 128], BF16, kind="ExternalInput")
    d_attn = nc.dram_tensor("attn_scr", [T, D], F32, kind="Internal")
    d_rbc = nc.dram_tensor("rbc_scr", [2, 512], F32, kind="Internal")
    d_y = nc.dram_tensor("y", [T, D], F32, kind="ExternalOutput")

    cms = {}

    def alloc(name, shape, dtype):
        cm = nc.sbuf_tensor(name, shape, dtype)
        h = cm.__enter__()
        cms[name] = cm
        return h

    def free(*names):
        for n in names:
            cms.pop(n).__exit__(None, None, None)

    xload = alloc("xload", [128, 2 * D], F32)
    sums = alloc("sums", [128, ntt], F32)
    ssq = alloc("ssq", [128, ntt], F32)
    mcol = alloc("mcol", [128, ntt], F32)
    rstd = alloc("rstd", [128, ntt], F32)
    vtmp = alloc("vtmp", [128, 8], F32)
    epsb = alloc("epsb", [128, 1], F32)
    biasb = alloc("biasb", [128, 20], F32)
    scal3b = alloc("scal3b", [1, 3], F32)
    scalB = alloc("scalB", [128, 3], F32)
    ones1 = alloc("ones1", [1, 128], F32)
    gmean = alloc("gmean", [128, NIC], F32)
    gmeanb = alloc("gmeanb", [128, NIC], BF16)
    idn = alloc("idn_s", [128, 128], F32)
    idnb = alloc("idnb_s", [128, 128], BF16)
    qkm = alloc("qkm", [128, 64], F32)
    redqk = alloc("redqk", [64, 1], F32)
    crow = alloc("crow", [1, 64], F32)
    negc = alloc("negc", [128, NH], F32)
    afst = alloc("afst", [128, 2 * 512], F32)

    q_fm = alloc("q_fm", [128, NIC * T], BF16)
    k_fm = alloc("k_fm", [128, NIC * T], BF16)
    v_ag = alloc("v_ag", [128, ntt * (NH * 65)], BF16)
    maskb = alloc("maskb", [128, 4 * 512], BF16)
    otmp = alloc("otmp", [64, 2 * 512], BF16)
    recips = alloc("recips", [1, 2 * 512], F32)
    rbcast = alloc("rbcast", [64, 2 * 512], F32)

    scr = alloc("scr", [128, D], F32)
    scr2 = alloc("scr2x", [128, 64], F32)
    cosb = alloc("cosb_s", [128, ntt * 32], F32)
    sinb = alloc("sinb_s", [128, ntt * 32], F32)
    xn_tm = alloc("xn_tm", [128, D], BF16)
    xn_fm = alloc("xn_fm", [128, NIC * T], BF16)
    wstage = alloc("wstage", [128, D], F32)
    wcast = alloc("wcast", [128, D], BF16)
    wT = alloc("wT", [128, 2 * NIC * 512], BF16)
    ropetm = alloc("ropetm", [128, 2 * 512], BF16)
    qss = alloc("qss", [128, NH * ntt], F32)
    kss = alloc("kss", [128, NH * ntt], F32)

    free("kss", "qss", "ropetm", "wT", "wcast", "wstage", "xn_fm", "xn_tm",
         "sinb_s", "cosb_s", "scr2x", "scr")
    ptb = alloc("ptb", [128, nkt * 512], BF16)
    kaug = alloc("kaug", [65, 2 * nkt * 128], BF16)
    qaug = alloc("qaug", [65, 8 * 512], BF16)
    dumpW = alloc("dumpW", [128, 2 * 512], BF16)
    sumch = alloc("sumch", [128, 128], F32)
    sumq = alloc("sumq", [128, NH * ntt], F32)
    cqb = alloc("cqb", [128, NH * ntt], F32)
    maskSb = alloc("maskSb", [128, 128], F32)
    negc8 = alloc("negc8", [128, NH], F32)
    negcd8 = alloc("negcd8", [128, NH], F32)
    woutT = alloc("woutT", [128, NIC * D], BF16)
    wstage2 = alloc("wstage2", [128, 2 * D], F32)
    wcast2 = alloc("wcast2", [128, D], BF16)
    astage = alloc("astage", [128, 4 * D], F32)

    free("astage", "wcast2", "wstage2", "woutT",
         "negcd8", "negc8", "maskSb", "cqb", "sumq", "sumch", "dumpW", "qaug", "kaug", "ptb",
         "rbcast", "recips", "otmp", "maskb", "v_ag", "k_fm", "q_fm")
    state = alloc("state", [128, ntt * D], F32)
    sbtm = alloc("sbtm", [128, 2 * D], BF16)
    sfm = alloc("sfm", [128, 2 * NIC * 512], BF16)
    h1b = alloc("h1b", [128, 8 * 512], BF16)
    h2b = alloc("h2b", [128, 4 * 512], BF16)
    h3b = alloc("h3b", [128, 2 * 512], BF16)
    tfb = alloc("tfb", [128, ntt], F32)
    tfsg = alloc("tfsg", [128, ntt], F32)
    coup = alloc("coup", [128, ntt], F32)
    ncoup = alloc("ncoup", [128, ntt], F32)
    tmpq = alloc("tmpq", [128, 512], F32)
    gb1 = alloc("gb1", [128, 4], BF16)
    gb2 = alloc("gb2", [128, 2], BF16)
    ab2 = alloc("ab2", [1, 8], F32)
    abb = alloc("abb", [128, 2], F32)
    cn1b = alloc("cn1b", [128, 8 * D], BF16)
    cn2b = alloc("cn2b", [128, 8 * 512], BF16)
    cn3b = alloc("cn3b", [128, 4 * 256], BF16)
    cn4b = alloc("cn4b", [128, 2], BF16)
    gc1b = alloc("gc1b", [128, 8 * 512], BF16)
    gc2b = alloc("gc2b", [128, 4 * 256], BF16)
    gc3b = alloc("gc3b", [128, 2], BF16)
    wstage3 = alloc("wstage3", [128, 2 * D], F32)

    pcms = []

    def palloc(name, shape, dtype):
        cm = nc.psum_tensor(name, shape, dtype)
        h = cm.__enter__()
        pcms.append(cm)
        return h

    pp = [palloc(f"pp{i}", [128, 1024], F32) for i in range(3)]
    p6t = palloc("p6", [128, 1024], BF16)
    p7t = palloc("p7", [128, 1024], BF16)

    class PV:
        def __init__(self, t, base):
            self.t = t
            self.base = base

        def __getitem__(self, key):
            if not isinstance(key, tuple):
                return self.t[:, self.base:self.base + 512]
            rows, cols = key
            if cols == slice(None):
                cols = slice(0, 512)
            return self.t[rows, self.base + (cols.start or 0): self.base + cols.stop]

    pv8 = [PV(pp[0], 0), PV(pp[0], 512), PV(pp[1], 0), PV(pp[1], 512),
           PV(pp[2], 0), PV(pp[2], 512)]
    p = pv8 + [p6t, p7t]

    r_xload = [Res(), Res()]
    r_sums = [Res(), Res()]
    r_ssq = [Res(), Res()]
    r_mcol = [Res(), Res()]
    r_rstd = [Res(), Res()]
    r_vtmp = [Res(), Res()]
    r_epsb = Res()
    r_cos, r_sin, r_biasb, r_scal3, r_scalB, r_ones1 = (Res() for _ in range(6))
    r_gmean, r_gmeanb, r_idn, r_idnb, r_scr, r_scr2 = (Res() for _ in range(6))
    r_qkm, r_redqk, r_crow, r_negc = (Res() for _ in range(4))
    r_xn_tm = Res()
    r_xn_fm = Res()
    r_wstage = Res()
    r_wcast = Res()
    r_wT = [Res(), Res()]
    r_ropetm = [Res(), Res()]
    r_qss, r_kss = Res(), Res()
    r_qh = [[Res() for _ in range(T // 512)] for _ in range(NH)]
    r_kh = [Res() for _ in range(NH)]
    r_vag = Res()
    r_pt = [Res() for _ in range(nkt)]
    r_maskb = Res()
    r_recips = [Res(), Res()]
    r_rbc_d = [Res(), Res()]
    r_rbcast = [Res(), Res()]
    r_otmp = [Res(), Res()]
    r_p = [Res() for _ in range(8)]
    r_woutT, r_wcast2 = Res(), Res()
    r_kaug = [Res(), Res()]
    r_qaug = [Res() for _ in range(8)]
    r_dumpW = [Res(), Res()]
    r_sumch = [[Res() for _ in range(16)], [Res() for _ in range(16)]]
    r_sumq, r_maskSb = Res(), Res()
    r_cqb = [Res() for _ in range(NH)]
    r_negc8, r_negcd8 = Res(), Res()
    r_wstage2 = [Res(), Res()]
    r_afst = [Res(), Res()]
    r_astage = Res()
    r_attnd = [Res() for _ in range(ntt)]
    r_state = [Res() for _ in range(ntt)]
    r_sbtm = Res()
    r_sfm = [Res(), Res()]
    r_h1, r_h2, r_h3 = (Res() for _ in range(3))
    r_tfb, r_tfsg, r_coup, r_ncoup = (Res() for _ in range(4))
    r_tmpq = [Res(), Res()]
    r_gb1, r_gb2, r_ab2, r_abb = (Res() for _ in range(4))
    r_cn1b, r_cn2b, r_cn3b, r_cn4b = (Res() for _ in range(4))
    r_gc1b, r_gc2b, r_gc3b = (Res() for _ in range(3))
    r_wstage3 = [Res(), Res()]
    r_y = [Res() for _ in range(ntt)]

    ctx = Ctx()

    def script():
        E = ctx.emit

        E("D", lambda e: e.dma_start(cosb[:], d_cos[:]), writes=[r_cos])
        E("D", lambda e: e.dma_start(sinb[:], d_sin[:]), writes=[r_sin])
        E("D", lambda e: e.dma_start(biasb[:], d_biasp[:]), writes=[r_biasb])
        E("D", lambda e: e.dma_start(scal3b[:], d_scal3[:]), writes=[r_scal3])
        E("D", lambda e: e.dma_start(maskb[:], d_mask[:]), writes=[r_maskb])
        E("D", lambda e: e.dma_start(idn[:], d_idn[:]), writes=[r_idn])
        E("D", lambda e: e.dma_start(idnb[:], d_idnb[:]), writes=[r_idnb])
        E("V", lambda e: e.memset(epsb[:], 1e-5), writes=[r_epsb])
        E("V", lambda e: e.memset(ones1[:], 1.0), writes=[r_ones1])
        for tt in range(ntt):
            E("V", lambda e, tt=tt: e.memset(
                bass.AP(v_ag, tt * (NH * 65) + 64, [[ntt * NH * 65, 128], [65, NH], [1, 1]]), 1.0),
              writes=[r_vag])
        E("T", lambda e: e.matmul(p[5][0:128, 0:3], ones1[:], scal3b[:], start=True, stop=True),
          reads=[r_ones1, r_scal3], writes=[r_p[5]])
        E("V", lambda e: e.tensor_copy(scalB[:], p[5][0:128, 0:3]), reads=[r_p[5]], writes=[r_scalB])

        def ln_tile(tt):
            xb = tt % 2
            E("D", lambda e, tt=tt, xb=xb: e.dma_start(
                xload[:, xb * D:(xb + 1) * D], d_x[tt * 128:(tt + 1) * 128, :]), writes=[r_xload[xb]])
            v0 = xb * 4
            E("V", lambda e, tt=tt, xb=xb: e.reduce_sum(
                sums[:, tt:tt + 1], xload[:, xb * D:(xb + 1) * D], axis=AX.X),
              reads=[r_xload[xb]], writes=[r_sums[xb]])
            E("S", lambda e, tt=tt, xb=xb: e.activation(
                afst[:, 0:D], xload[:, xb * D:(xb + 1) * D], AF.Square, accum_out=ssq[:, tt:tt + 1]),
              reads=[r_xload[xb]], writes=[r_afst[0], r_afst[1], r_ssq[xb]])
            E("V", lambda e, tt=tt, xb=xb: e.tensor_scalar(
                mcol[:, tt:tt + 1], sums[:, tt:tt + 1], 1.0 / D, None, ALU.mult),
              reads=[r_sums[xb]], writes=[r_mcol[xb]])
            E("V", lambda e, tt=tt, v0=v0, xb=xb: e.tensor_mul(
                vtmp[:, v0:v0 + 1], mcol[:, tt:tt + 1], mcol[:, tt:tt + 1]),
              reads=[r_mcol[xb]], writes=[r_vtmp[xb]])
            E("V", lambda e, tt=tt, v0=v0, xb=xb: e.tensor_scalar(
                vtmp[:, v0 + 1:v0 + 2], ssq[:, tt:tt + 1], 1.0 / D, None, ALU.mult),
              reads=[r_ssq[xb]], writes=[r_vtmp[xb]])
            E("V", lambda e, v0=v0, xb=xb: e.tensor_sub(
                vtmp[:, v0 + 2:v0 + 3], vtmp[:, v0 + 1:v0 + 2], vtmp[:, v0:v0 + 1]),
              reads=[r_vtmp[xb]], writes=[r_vtmp[xb]])
            E("S", lambda e, v0=v0, xb=xb: e.activation(
                vtmp[:, v0 + 3:v0 + 4], vtmp[:, v0 + 2:v0 + 3], AF.Sqrt, bias=epsb[:]),
              reads=[r_vtmp[xb], r_epsb], writes=[r_vtmp[xb]])
            E("V", lambda e, tt=tt, v0=v0, xb=xb: e.reciprocal(
                rstd[:, tt:tt + 1], vtmp[:, v0 + 3:v0 + 4]),
              reads=[r_vtmp[xb]], writes=[r_rstd[xb]])
            E("V", lambda e, tt=tt, xb=xb: e.tensor_scalar(
                xn_tm[:], xload[:, xb * D:(xb + 1) * D],
                mcol[:, tt:tt + 1], rstd[:, tt:tt + 1], ALU.subtract, ALU.mult),
              reads=[r_xload[xb], r_mcol[xb], r_rstd[xb]], writes=[r_xn_tm])
            for half in range(2):
                pb, rb = p[6 + half], r_p[6 + half]
                for j in range(4):
                    ic = half * 4 + j
                    E("T", lambda e, ic=ic, j=j, pb=pb: e.transpose(
                        pb[:, j * 128:(j + 1) * 128], xn_tm[:, ic * 128:(ic + 1) * 128], idnb[:]),
                      reads=[r_xn_tm, r_idnb], writes=[rb])
                E("S", lambda e, tt=tt, half=half, pb=pb: e.copy(
                    bass.AP(xn_fm, half * 4 * T + tt * 128, [[NIC * T, 128], [T, 4], [1, 128]]),
                    bass.AP(pb, 0, [[1024, 128], [128, 4], [1, 128]])),
                  reads=[rb], writes=[r_xn_fm])

        for tt in range(min(6, ntt)):
            ln_tile(tt)
        lnq = list(range(6, ntt))

        def emit_gmean():
            for ic in range(NIC):
                E("V", lambda e, ic=ic: e.reduce_sum(
                    gmean[:, ic:ic + 1], xn_fm[:, ic * T:(ic + 1) * T], axis=AX.X),
                  reads=[r_xn_fm], writes=[r_gmean])
            E("V", lambda e: e.tensor_scalar(gmean[:], gmean[:], 1.0 / T, None, ALU.mult),
              reads=[r_gmean], writes=[r_gmean])
            E("V", lambda e: e.tensor_copy(gmeanb[:], gmean[:]), reads=[r_gmean], writes=[r_gmeanb])

        def build_wT(oc):
            ops = []
            wb = oc % 2
            for os_ in range(4):
                ops.append(lambda oc=oc, os_=os_: E("D", lambda e, oc=oc, os_=os_: e.dma_start(
                    wstage[:], d_wqkv[oc * 512 + os_ * 128: oc * 512 + (os_ + 1) * 128, :]),
                  writes=[r_wstage]))
                ops.append(lambda: E("V", lambda e: e.tensor_copy(wcast[:], wstage[:]),
                  reads=[r_wstage], writes=[r_wcast]))
                for half in range(2):
                    pb, rb = p[6 + half], r_p[6 + half]

                    def tr_ev(half=half, os_=os_, pb=pb, rb=rb, wb=wb):
                        for j in range(4):
                            ic = half * 4 + j
                            E("T", lambda e, ic=ic, j=j, pb=pb: e.transpose(
                                pb[:, j * 128:(j + 1) * 128], wcast[:, ic * 128:(ic + 1) * 128], idnb[:]),
                              reads=[r_wcast, r_idnb], writes=[rb])
                        E("S", lambda e, wb=wb, half=half, os_=os_, pb=pb: e.copy(
                            bass.AP(wT, wb * NIC * 512 + half * 4 * 512 + os_ * 128,
                                    [[2 * NIC * 512, 128], [512, 4], [1, 128]]),
                            bass.AP(pb, 0, [[1024, 128], [128, 4], [1, 128]])),
                          reads=[rb], writes=[r_wT[wb]])
                    ops.append(tr_ev)
            return ops

        for op in build_wT(0):
            op()
        for oc in range(6):
            wb = oc % 2
            wbq = build_wT(oc + 1) if oc < 5 else []
            pend = [None]

            def flush_pend():
                if pend[0] is not None:
                    for f in pend[0]:
                        f()
                    pend[0] = None

            for tt in range(ntt):
                pq, rq = p[tt % 4], r_p[tt % 4]
                for ic in range(NIC):
                    E("T", lambda e, tt=tt, ic=ic, wb=wb, pq=pq: e.matmul(
                        pq[:], xn_fm[:, ic * T + tt * 128: ic * T + (tt + 1) * 128],
                        wT[:, wb * NIC * 512 + ic * 512: wb * NIC * 512 + (ic + 1) * 512],
                        start=(ic == 0), stop=(ic == NIC - 1)),
                      reads=[r_xn_fm, r_wT[wb]], writes=[rq])
                flush_pend()
                for _ in range(3):
                    if wbq:
                        wbq.pop(0)()
                if oc == 0 and lnq:
                    ln_tile(lnq.pop(0))
                if oc < 4:
                    rb_ = tt % 2

                    def ev(base, pq=pq):
                        return bass.AP(pq.t, pq.base + base, [[1024, 128], [64, 8], [2, 32]])

                    def evr(base, rb_=rb_):
                        return bass.AP(ropetm, rb_ * 512 + base, [[2 * 512, 128], [64, 8], [2, 32]])

                    def cosap(tt=tt):
                        return bass.AP(cosb, tt * 32, [[ntt * 32, 128], [0, 8], [1, 32]])

                    def sinap(tt=tt):
                        return bass.AP(sinb, tt * 32, [[ntt * 32, 128], [0, 8], [1, 32]])

                    def scrap(k):
                        return bass.AP(scr, k * 256, [[D, 128], [32, 8], [1, 32]])

                    E("V", lambda e, ev=ev, cosap=cosap, scrap=scrap: e.tensor_mul(scrap(0), ev(0), cosap()),
                      reads=[rq, r_cos], writes=[r_scr])
                    E("V", lambda e, ev=ev, sinap=sinap, scrap=scrap: e.tensor_mul(scrap(1), ev(1), sinap()),
                      reads=[rq, r_sin], writes=[r_scr])
                    E("V", lambda e, ev=ev, sinap=sinap, scrap=scrap: e.tensor_mul(scrap(2), ev(0), sinap()),
                      reads=[rq, r_sin], writes=[r_scr])
                    E("V", lambda e, ev=ev, cosap=cosap, scrap=scrap: e.tensor_mul(scrap(3), ev(1), cosap()),
                      reads=[rq, r_cos], writes=[r_scr])
                    E("V", lambda e, evr=evr, scrap=scrap, rb_=rb_: e.tensor_sub(evr(0), scrap(0), scrap(1)),
                      reads=[r_scr], writes=[r_ropetm[rb_]])
                    E("V", lambda e, evr=evr, scrap=scrap, rb_=rb_: e.tensor_add(evr(1), scrap(2), scrap(3)),
                      reads=[r_scr], writes=[r_ropetm[rb_]])
                    def mk_pend(tt=tt, rb_=rb_, oc=oc):
                        fs = []
                        pb, rb2 = p[6 + (tt % 2)], r_p[6 + (tt % 2)]
                        for j in range(4):
                            def f(j=j, rb_=rb_, pb=pb, rb2=rb2):
                                E("T", lambda e, rb_=rb_, j=j, pb=pb: e.transpose(
                                    pb[:, j * 128:(j + 1) * 128],
                                    ropetm[:, rb_ * 512 + j * 128: rb_ * 512 + (j + 1) * 128], idnb[:]),
                                  reads=[r_ropetm[rb_], r_idnb], writes=[rb2])
                            fs.append(f)
                        dst = q_fm if oc < 2 else k_fm
                        if oc < 2:
                            rdst = [rr for j in range(4) for dj in range(2)
                                    for rr in r_qh[(oc % 2) * 8 + 2 * j + dj][tt // 4: tt // 4 + 1]]
                        else:
                            rdst = [r_kh[(oc % 2) * 8 + 2 * j + dj] for j in range(4) for dj in range(2)]
                        ch0 = (oc % 2) * 4

                        def fev(tt=tt, ch0=ch0, pb=pb, rb2=rb2, dst=dst, rdst=rdst):
                            E("S", lambda e, tt=tt, ch0=ch0, pb=pb, dst=dst: e.copy(
                                bass.AP(dst, ch0 * T + tt * 128, [[NIC * T, 128], [T, 4], [1, 128]]),
                                bass.AP(pb, 0, [[1024, 128], [128, 4], [1, 128]])),
                              reads=[rb2], writes=rdst)
                        fs.append(fev)
                        return fs

                    pend[0] = mk_pend()
                else:
                    h0 = (oc - 4) * 8
                    E("S", lambda e, tt=tt, h0=h0, pq=pq: e.copy(
                        bass.AP(v_ag, tt * (NH * 65) + h0 * 65, [[ntt * NH * 65, 128], [65, 8], [1, 64]]),
                        bass.AP(pq.t, pq.base, [[1024, 128], [64, 8], [1, 64]])),
                      reads=[rq], writes=[r_vag])
            flush_pend()
            for op in wbq:
                op()
            if oc == 0:
                while lnq:
                    ln_tile(lnq.pop(0))
                emit_gmean()

        ctx.barrier()
        E("D", lambda e: e.dma_start(maskSb[:], d_masksq[:]), writes=[r_maskSb])
        E("V", lambda e: e.memset(kaug[64:65, :], 1.0), writes=[r_kaug[0], r_kaug[1]])
        for hp2 in range(2):
            for qt in range(ntt):
                npair = ((qt * 128) // 512 + 1) // 2
                pads = [(1, -3.0e4)]
                if npair == 0 or (npair == 1 and qt % 2 == 1):
                    pads.append((2, -3.0e4))
                if npair == 0 or (npair == 1 and qt % 2 == 0):
                    pads.append((3, 1e-30))
                for slot, val in pads:
                    E("V", lambda e, hp2=hp2, qt=qt, slot=slot, val=val: e.memset(
                        sumch[:, hp2 * 64 + qt * 4 + slot: hp2 * 64 + qt * 4 + slot + 1], val),
                      writes=[r_sumch[hp2][qt]])

        wprep = []

        def wadd(eng, fn, rd, wr):
            wprep.append((eng, fn, rd, wr))

        for os_ in range(8):
            sb = os_ % 2
            wadd("D", lambda e, os_=os_, sb=sb: e.dma_start(
                wstage2[:, sb * D:(sb + 1) * D], d_wout[os_ * 128:(os_ + 1) * 128, :]),
                [], [r_wstage2[sb]])
            wadd("V", lambda e, sb=sb: e.tensor_copy(wcast2[:], wstage2[:, sb * D:(sb + 1) * D]),
                 [r_wstage2[sb]], [r_wcast2])
            for half in range(2):
                pb, rb = p[6 + half], r_p[6 + half]
                for j in range(4):
                    dc = half * 4 + j
                    wadd("T", lambda e, dc=dc, j=j, pb=pb: e.transpose(
                        pb[:, j * 128:(j + 1) * 128], wcast2[:, dc * 128:(dc + 1) * 128], idnb[:]),
                        [r_wcast2, r_idnb], [rb])
                wadd("V", lambda e, os_=os_, half=half, pb=pb: e.tensor_copy(
                    bass.AP(woutT, half * 4 * D + os_ * 128, [[NIC * D, 128], [D, 4], [1, 128]]),
                    bass.AP(pb, 0, [[1024, 128], [128, 4], [1, 128]])),
                    [rb], [r_woutT])
        wi = [0]

        def drip(n):
            for _ in range(n):
                if wi[0] < len(wprep):
                    eng, fn, rd, wr = wprep[wi[0]]
                    wi[0] += 1
                    E(eng, fn, reads=rd, writes=wr)

        tail1 = [None]
        tail2 = [None]

        def flush_tail1():
            if tail1[0] is not None:
                for f in tail1[0]:
                    f()
                tail1[0] = None

        def flush_tail():
            flush_tail1()
            if tail2[0] is not None:
                for f in tail2[0]:
                    f()
                tail2[0] = None

        pend_pv = []

        def _pop_pv():
            kt, fn, bd = pend_pv.pop(0)
            if bd:
                flush_tail()
            fn()

        def drain_pv(min_kt):
            while pend_pv and pend_pv[0][0] >= min_kt:
                _pop_pv()

        def drain_pv_n(n):
            for _ in range(n):
                if pend_pv:
                    _pop_pv()

        def gen_passA(h):
            ch = h // 2
            hb = (h % 2) * 64
            kslot = (h % 2) * nkt * 128
            hp = h % 2
            for kt in range(nkt):
                E("G", lambda e, kt=kt, ch=ch, hb=hb, kslot=kslot: e.tensor_copy(
                    kaug[0:64, kslot + kt * 128: kslot + (kt + 1) * 128],
                    k_fm[hb:hb + 64, ch * T + kt * 128: ch * T + (kt + 1) * 128]),
                  reads=[r_kh[h]], writes=[r_kaug[h % 2]])
            for qc in range(nqc):
                sl = (h % 2) * 4 + qc
                E("G", lambda e, qc=qc, ch=ch, hb=hb, sl=sl: e.tensor_copy(
                    qaug[0:64, sl * 512:(sl + 1) * 512],
                    q_fm[hb:hb + 64, ch * T + qc * 512: ch * T + (qc + 1) * 512]),
                  reads=[r_qh[h][qc]], writes=[r_qaug[sl]])
            yield
            for gi in range(ntt):
                qt = gi
                nfull = (qt * 128) // 512
                rem = qt * 128 - nfull * 512
                k0, kw = nfull * 512, rem + 128
                ps, rs = p[gi % 4], r_p[gi % 4]
                E("T", lambda e, qt=qt, k0=k0, kw=kw, ps=ps, ch=ch, hb=hb: e.matmul(
                    ps[:, 0:kw], q_fm[hb:hb + 64, ch * T + qt * 128: ch * T + (qt + 1) * 128],
                    k_fm[hb:hb + 64, ch * T + k0: ch * T + k0 + kw],
                    start=True, stop=True),
                  reads=[r_qh[h][qt // 4], r_kh[h]], writes=[rs])
                E("V", lambda e, ps=ps, rem=rem: e.tensor_add(
                    ps[:, rem:rem + 128], ps[:, rem:rem + 128], maskSb[:]),
                  reads=[r_maskSb, rs], writes=[rs])
                E("V", lambda e, ps=ps, kw=kw, qt=qt, hp=hp: e.reduce_max(
                    sumch[:, hp * 64 + qt * 4: hp * 64 + qt * 4 + 1],
                    ps[:, 0:kw], axis=AX.X),
                  reads=[rs], writes=[r_sumch[hp][qt]])
                if gi == 7:
                    flush_tail()
                if gi % 2 == 1:
                    if gi >= 9:
                        drain_pv_n(2)
                    yield
            pairi = 0
            for qt in range(ntt):
                nfull = (qt * 128) // 512
                npair = (nfull + 1) // 2
                fi = 0
                pq = 0
                while fi < nfull:
                    take = [(f * 512, 512) for f in (fi, fi + 1) if f < nfull]
                    fi += len(take)
                    pair = pairi % 2
                    pairi += 1
                    drain_pv_n(1)
                    halves = []
                    for hi, (k0, kw) in enumerate(take):
                        ps, rs = p[2 * pair + hi], r_p[2 * pair + hi]
                        E("T", lambda e, qt=qt, k0=k0, kw=kw, ps=ps, ch=ch, hb=hb: e.matmul(
                            ps[:, 0:kw], q_fm[hb:hb + 64, ch * T + qt * 128: ch * T + (qt + 1) * 128],
                            k_fm[hb:hb + 64, ch * T + k0: ch * T + k0 + kw],
                            start=True, stop=True),
                          reads=[r_qh[h][qt // 4], r_kh[h]], writes=[rs])
                        halves.append((ps, rs, kw))
                    w_tot = 1024 if len(halves) == 2 else 512
                    onS = (pq == 1) or (npair == 1 and qt % 2 == 1)
                    pq += 1
                    if onS:
                        E("S", lambda e, w_tot=w_tot, qt=qt, hp=hp, pair=pair: e.activation(
                            dumpW[:, 0:w_tot], pp[pair][:, 0:w_tot], AF.Exp,
                            scale=1.0 / 64.0,
                            accum_out=sumch[:, hp * 64 + qt * 4 + 3: hp * 64 + qt * 4 + 4]),
                          reads=[halves[0][1], halves[-1][1]],
                          writes=[r_dumpW[0], r_sumch[hp][qt]])
                    else:
                        E("V", lambda e, w_tot=w_tot, qt=qt, hp=hp, pair=pair: e.reduce_max(
                            sumch[:, hp * 64 + qt * 4 + 2: hp * 64 + qt * 4 + 3],
                            pp[pair][:, 0:w_tot], axis=AX.X),
                          reads=[halves[0][1], halves[-1][1]],
                          writes=[r_sumch[hp][qt]])
                    yield
            drain_pv(-1)
            rall = [r_sumch[hp][qt] for qt in range(ntt)]

            def sc_ap(slot):
                return bass.AP(sumch, hp * 64 + slot, [[128, 128], [4, ntt]])

            cq = cqb
            E("V", lambda e, h=h: e.tensor_max(
                sumq[:, h * ntt:(h + 1) * ntt], sc_ap(0), sc_ap(1)),
              reads=rall, writes=[r_sumq])
            E("V", lambda e, h=h: e.tensor_max(
                sumq[:, h * ntt:(h + 1) * ntt], sumq[:, h * ntt:(h + 1) * ntt], sc_ap(2)),
              reads=rall + [r_sumq], writes=[r_sumq])
            E("S", lambda e, h=h: e.activation(
                cq[:, h * ntt:(h + 1) * ntt], sc_ap(3), AF.Ln),
              reads=rall, writes=[r_cqb[h]])
            E("V", lambda e, h=h: e.scalar_tensor_tensor(
                cq[:, h * ntt:(h + 1) * ntt], cq[:, h * ntt:(h + 1) * ntt], 64.0,
                sumq[:, h * ntt:(h + 1) * ntt], ALU.mult, ALU.max),
              reads=[r_cqb[h], r_sumq], writes=[r_cqb[h]])
            E("V", lambda e, h=h: e.tensor_scalar(
                cq[:, h * ntt:(h + 1) * ntt], cq[:, h * ntt:(h + 1) * ntt],
                -1.0, None, ALU.mult),
              reads=[r_cqb[h]], writes=[r_cqb[h]])
            yield

        def gen_passB(h):
            ch = h // 2
            hb = (h % 2) * 64
            kslot = (h % 2) * nkt * 128
            for qc in range(nqc):
                drip(3)
                nk = 4 * qc + 4
                for j in range(4):
                    col = h * ntt + qc * 4 + j
                    E("T", lambda e, col=col, j=j: e.transpose(
                        pp[2][0:1, 512 + j * 128: 512 + j * 128 + 128], cqb[:, col:col + 1],
                        idn[:]),
                      reads=[r_cqb[h], r_idn], writes=[r_p[5]])
                sl = (h % 2) * 4 + qc
                E("V", lambda e, sl=sl: e.tensor_copy(
                    qaug[64:65, sl * 512:(sl + 1) * 512], pp[2][0:1, 512:1024]),
                  reads=[r_p[5]], writes=[r_qaug[sl]])
                flush_tail1()
                yield
                prs = list(range(nk // 2 - 1, -1, -1))
                for pi, pr in enumerate(prs):
                    drain_pv(2 * pr)
                    pair = pi % 2
                    for hi in range(2):
                        kt = 2 * pr + hi
                        ps, rs = p[2 * pair + hi], r_p[2 * pair + hi]
                        E("T", lambda e, kt=kt, ps=ps, sl=sl, kslot=kslot: e.matmul(
                            ps[:], kaug[:, kslot + kt * 128: kslot + (kt + 1) * 128],
                            qaug[:, sl * 512:(sl + 1) * 512],
                            start=True, stop=True),
                          reads=[r_kaug[h % 2], r_qaug[sl]], writes=[rs])
                    if pi == 1 or (pi == 0 and len(prs) == 1):
                        flush_tail()
                    kt0 = 2 * pr
                    E("S", lambda e, kt0=kt0, pair=pair: e.activation(
                        ptb[:, kt0 * 512:(kt0 + 2) * 512], pp[pair][:, 0:1024], AF.Exp, scale=0.125),
                      reads=[r_p[2 * pair], r_p[2 * pair + 1]],
                      writes=[r_pt[kt0], r_pt[kt0 + 1]])
                    if kt0 + 1 >= 4 * qc:
                        m0 = kt0 - 4 * qc
                        if m0 >= 0:
                            E("G", lambda e, kt0=kt0, m0=m0: e.tensor_mul(
                                ptb[:, kt0 * 512:(kt0 + 2) * 512], ptb[:, kt0 * 512:(kt0 + 2) * 512],
                                maskb[:, m0 * 512:(m0 + 2) * 512]),
                              reads=[r_pt[kt0], r_pt[kt0 + 1], r_maskb],
                              writes=[r_pt[kt0], r_pt[kt0 + 1]])
                        else:
                            E("G", lambda e, kt0=kt0: e.tensor_mul(
                                ptb[:, (kt0 + 1) * 512:(kt0 + 2) * 512],
                                ptb[:, (kt0 + 1) * 512:(kt0 + 2) * 512], maskb[:, 0:512]),
                              reads=[r_pt[kt0 + 1], r_maskb], writes=[r_pt[kt0 + 1]])
                    yield
                drain_pv(-1)
                po, ro = p[4], r_p[4]
                for ki, kt in enumerate(range(nk - 1, -1, -1)):
                    def fpv(e=None, h=h, kt=kt, po=po, ki=ki, nk=nk, ro=ro):
                        E("T", lambda e, h=h, kt=kt, po=po, ki=ki, nk=nk: e.matmul(
                            po[0:65, 0:512],
                            v_ag[:, kt * (NH * 65) + h * 65: kt * (NH * 65) + (h + 1) * 65],
                            ptb[:, kt * 512:(kt + 1) * 512], start=(ki == 0), stop=(ki == nk - 1)),
                          reads=[r_vag, r_pt[kt]], writes=[ro])
                    pend_pv.append((kt, fpv, ki == 0))
                rp = qc % 2

                def mk_tail1(rp=rp, po=po, ro=ro):
                    def t0():
                        E("V", lambda e: e.reciprocal(
                            recips[0:1, rp * 512:(rp + 1) * 512], po[64:65, 0:512]),
                          reads=[ro], writes=[r_recips[rp]])
                        E("D", lambda e: e.dma_start(
                            d_rbc[rp:rp + 1, :], recips[0:1, rp * 512:(rp + 1) * 512]),
                          reads=[r_recips[rp]], writes=[r_rbc_d[rp]])
                    return [t0]

                def mk_tail2(h=h, qc=qc, rp=rp, ch=ch, hb=hb, po=po, ro=ro):
                    def t1():
                        E("D", lambda e: e.dma_start(
                            rbcast[:, rp * 512:(rp + 1) * 512],
                            bass.AP(d_rbc, rp * 512, [[0, 64], [1, 512]])),
                          reads=[r_rbc_d[rp]], writes=[r_rbcast[rp]])

                    def t2():
                        E("S", lambda e: e.copy(
                            otmp[:, rp * 512:(rp + 1) * 512], po[0:64, 0:512]),
                          reads=[ro], writes=[r_otmp[rp]])

                    def t3():
                        E("V", lambda e: e.scalar_tensor_tensor(
                            q_fm[hb:hb + 64, ch * T + qc * 512: ch * T + (qc + 1) * 512],
                            otmp[:, rp * 512:(rp + 1) * 512], 1.0,
                            rbcast[:, rp * 512:(rp + 1) * 512],
                            ALU.mult, ALU.mult),
                          reads=[r_otmp[rp], r_rbcast[rp]], writes=[r_qh[h][qc]])
                    return [t1, t2, t3]

                def fin(mk1=mk_tail1, mk2=mk_tail2):
                    tail1[0] = mk1()
                    tail2[0] = mk2()
                pend_pv.append((-1, fin, True))
                if DRAIN_NOW:
                    drain_pv(-1)
                yield

        for h in range(NH):
            for _ in gen_passA(h):
                pass
            for _ in gen_passB(h):
                pass
        drain_pv(-1)
        flush_tail()
        drip(len(wprep))

        for tc in range(nqc):
            for oc in range(NIC):
                pw, rw = p[oc % 2], r_p[oc % 2]
                for dc in range(NIC):
                    E("T", lambda e, oc=oc, tc=tc, dc=dc, pw=pw: e.matmul(
                        pw[:], woutT[:, dc * D + oc * 128: dc * D + (oc + 1) * 128],
                        q_fm[:, dc * T + tc * 512: dc * T + (tc + 1) * 512],
                        start=(dc == 0), stop=(dc == NIC - 1)),
                      reads=[r_woutT, r_qh[2 * dc][tc], r_qh[2 * dc + 1][tc]], writes=[rw])
                sb = oc % 2
                E("V", lambda e, pw=pw, sb=sb: e.tensor_copy(afst[:, sb * 512:(sb + 1) * 512], pw[:]),
                  reads=[rw], writes=[r_afst[sb]])
                pt_, rt_ = p[2 + sb], r_p[2 + sb]
                for j in range(4):
                    E("T", lambda e, sb=sb, j=j, pt_=pt_: e.transpose(
                        pt_[:, j * 128:(j + 1) * 128],
                        afst[:, sb * 512 + j * 128: sb * 512 + (j + 1) * 128], idn[:]),
                      reads=[r_afst[sb], r_idn], writes=[rt_])
                for j in range(4):
                    E("V", lambda e, oc=oc, j=j, pt_=pt_: e.tensor_copy(
                        astage[:, j * D + oc * 128: j * D + (oc + 1) * 128],
                        pt_[:, j * 128:(j + 1) * 128]),
                      reads=[rt_], writes=[r_astage])
            for j in range(4):
                tt = tc * 4 + j
                E("D", lambda e, tt=tt, j=j: e.dma_start(
                    d_attn[tt * 128:(tt + 1) * 128, :], astage[:, j * D:(j + 1) * D]),
                  reads=[r_astage], writes=[r_attnd[tt]])

        ctx.barrier()

        b0q = []

        def q(fn):
            b0q.append(fn)

        def cast_load(dram, rows, cols, dst, rdst, dst_cols, defer=False):
            for icx in range(rows // 128):
                sb = icx % 2

                def f(icx=icx, sb=sb, dram=dram, cols=cols, dst=dst, rdst=rdst,
                      dst_cols=dst_cols):
                    E("D", lambda e, icx=icx, sb=sb, dram=dram, cols=cols: e.dma_start(
                        wstage3[:, sb * D: sb * D + cols], dram[icx * 128:(icx + 1) * 128, :]),
                      writes=[r_wstage3[sb]])
                    E("V", lambda e, icx=icx, sb=sb, dst=dst, cols=cols, dst_cols=dst_cols: e.tensor_copy(
                        dst[:, icx * dst_cols: icx * dst_cols + cols], wstage3[:, sb * D: sb * D + cols]),
                      reads=[r_wstage3[sb]], writes=[rdst])
                if defer:
                    q(f)
                else:
                    f()

        def state0(tt):
            xb = tt % 2
            E("D", lambda e, tt=tt, xb=xb: e.dma_start(
                xload[:, xb * D:(xb + 1) * D], d_x[tt * 128:(tt + 1) * 128, :]),
              writes=[r_xload[xb]])
            E("V", lambda e, tt=tt, xb=xb: e.tensor_scalar(
                state[:, tt * D:(tt + 1) * D], xload[:, xb * D:(xb + 1) * D],
                mcol[:, tt:tt + 1], rstd[:, tt:tt + 1], ALU.subtract, ALU.mult),
              reads=[r_xload[xb], r_mcol[xb], r_rstd[xb]], writes=[r_state[tt]])

        for tt in range(min(4, ntt)):
            state0(tt)
        for tt in range(4, ntt):
            q(lambda tt=tt: state0(tt))
        cast_load(d_cn1, D, D, cn1b, r_cn1b, D)
        cast_load(d_cn2, D, 512, cn2b, r_cn2b, 512, defer=True)
        cast_load(d_gc1, D, 512, gc1b, r_gc1b, 512, defer=True)
        cast_load(d_cn3, 512, 256, cn3b, r_cn3b, 256, defer=True)
        cast_load(d_gc2, 512, 256, gc2b, r_gc2b, 256, defer=True)
        cast_load(d_cn4, 128, 2, cn4b, r_cn4b, 2, defer=True)
        cast_load(d_gc3, 128, 2, gc3b, r_gc3b, 2, defer=True)
        for oc in range(4):
            def f1(oc=oc):
                for ic in range(NIC):
                    E("T", lambda e, oc=oc, ic=ic: e.matmul(
                        p[5][:, oc:oc + 1], gc1b[:, ic * 512 + oc * 128: ic * 512 + (oc + 1) * 128],
                        gmeanb[:, ic:ic + 1], start=(ic == 0), stop=(ic == NIC - 1)),
                      reads=[r_gc1b, r_gmeanb], writes=[r_p[5]])
                E("S", lambda e, oc=oc: e.activation(
                    gb1[:, oc:oc + 1], p[5][:, oc:oc + 1], AF.Gelu, bias=biasb[:, 14 + oc:15 + oc]),
                  reads=[r_p[5], r_biasb], writes=[r_gb1])
            q(f1)
        for oc in range(2):
            def f2(oc=oc):
                for ic in range(4):
                    E("T", lambda e, oc=oc, ic=ic: e.matmul(
                        p[5][:, 8 + oc:9 + oc], gc2b[:, ic * 256 + oc * 128: ic * 256 + (oc + 1) * 128],
                        gb1[:, ic:ic + 1], start=(ic == 0), stop=(ic == 3)),
                      reads=[r_gc2b, r_gb1], writes=[r_p[5]])
                E("S", lambda e, oc=oc: e.activation(
                    gb2[:, oc:oc + 1], p[5][:, 8 + oc:9 + oc], AF.Gelu, bias=biasb[:, 18 + oc:19 + oc]),
                  reads=[r_p[5], r_biasb], writes=[r_gb2])
            q(f2)

        def f3():
            for ic in range(2):
                E("T", lambda e, ic=ic: e.matmul(
                    p[5][0:1, 12:13], gc3b[:, ic:ic + 1], gb2[:, ic:ic + 1],
                    start=(ic == 0), stop=(ic == 1)),
                  reads=[r_gc3b, r_gb2], writes=[r_p[5]])
            E("S", lambda e: e.activation(ab2[0:1, 2:3], p[5][0:1, 12:13], AF.Sigmoid, bias=scal3b[0:1, 1:2]),
              reads=[r_p[5], r_scal3], writes=[r_ab2])
            E("V", lambda e: e.tensor_scalar(ab2[0:1, 3:4], ab2[0:1, 2:3], 1.4, -1.0, ALU.mult, ALU.add),
              reads=[r_ab2], writes=[r_ab2])
            E("V", lambda e: e.tensor_mul(ab2[0:1, 4:5], ab2[0:1, 3:4], scal3b[0:1, 2:3]),
              reads=[r_ab2, r_scal3], writes=[r_ab2])
            E("V", lambda e: e.tensor_scalar(ab2[0:1, 0:1], ab2[0:1, 4:5], 1.0, 0.1, ALU.mult, ALU.add),
              reads=[r_ab2], writes=[r_ab2])
            E("V", lambda e: e.tensor_scalar(ab2[0:1, 1:2], scal3b[0:1, 2:3], 0.6, None, ALU.mult),
              reads=[r_scal3], writes=[r_ab2])
            E("T", lambda e: e.matmul(p[5][0:128, 16:18], ones1[:], ab2[0:1, 0:2], start=True, stop=True),
              reads=[r_ones1, r_ab2], writes=[r_p[5]])
            E("V", lambda e: e.tensor_copy(abb[:], p[5][0:128, 16:18]), reads=[r_p[5]], writes=[r_abb])
        q(f3)

        def drain_b0(n):
            for _ in range(n):
                if b0q:
                    b0q.pop(0)()

        def gen_ffn_A(cyc, tc):
            sp = tc % 2
            for j in range(4):
                tt = tc * 4 + j
                jb = j % 2
                E("V", lambda e, tt=tt, jb=jb: e.tensor_copy(
                    sbtm[:, jb * D:(jb + 1) * D], state[:, tt * D:(tt + 1) * D]),
                  reads=[r_state[tt]], writes=[r_sbtm])
                for half in range(2):
                    pb, rb = p[6 + half], r_p[6 + half]
                    for jj in range(4):
                        ic2 = half * 4 + jj
                        E("T", lambda e, jb=jb, ic2=ic2, jj=jj, pb=pb: e.transpose(
                            pb[:, jj * 128:(jj + 1) * 128],
                            sbtm[:, jb * D + ic2 * 128: jb * D + (ic2 + 1) * 128], idnb[:]),
                          reads=[r_sbtm, r_idnb], writes=[rb])
                    E("V", lambda e, half=half, j=j, pb=pb, sp=sp: e.tensor_copy(
                        bass.AP(sfm, sp * NIC * 512 + half * 4 * 512 + j * 128,
                                [[2 * NIC * 512, 128], [512, 4], [1, 128]]),
                        bass.AP(pb, 0, [[1024, 128], [128, 4], [1, 128]])),
                      reads=[rb], writes=[r_sfm[sp]])
                    yield

        def gen_ffn_B(cyc, tc):
            sp = tc % 2
            if True:
                for oc in range(8):
                    ph, rhh = p[oc % 4], r_p[oc % 4]
                    for ic in range(NIC):
                        E("T", lambda e, oc=oc, ic=ic, ph=ph, sp=sp: e.matmul(
                            ph[:], cn1b[:, ic * D + oc * 128: ic * D + (oc + 1) * 128],
                            sfm[:, sp * NIC * 512 + ic * 512: sp * NIC * 512 + (ic + 1) * 512],
                            start=(ic == 0), stop=(ic == NIC - 1)),
                          reads=[r_cn1b, r_sfm[sp]], writes=[rhh])
                    E("S", lambda e, oc=oc, ph=ph: e.activation(
                        h1b[:, oc * 512:(oc + 1) * 512], ph[:], AF.Gelu, bias=biasb[:, oc:oc + 1]),
                      reads=[rhh, r_biasb], writes=[r_h1])
                    yield
                for oc in range(4):
                    ph, rhh = p[oc % 4], r_p[oc % 4]
                    for ic in range(NIC):
                        E("T", lambda e, oc=oc, ic=ic, ph=ph: e.matmul(
                            ph[:], cn2b[:, ic * 512 + oc * 128: ic * 512 + (oc + 1) * 128],
                            h1b[:, ic * 512:(ic + 1) * 512], start=(ic == 0), stop=(ic == NIC - 1)),
                          reads=[r_cn2b, r_h1], writes=[rhh])
                    E("S", lambda e, oc=oc, ph=ph: e.activation(
                        h2b[:, oc * 512:(oc + 1) * 512], ph[:], AF.Gelu, bias=biasb[:, 8 + oc:9 + oc]),
                      reads=[rhh, r_biasb], writes=[r_h2])
                    yield
                for oc in range(2):
                    ph, rhh = p[oc % 4], r_p[oc % 4]
                    for ic in range(4):
                        E("T", lambda e, oc=oc, ic=ic, ph=ph: e.matmul(
                            ph[:], cn3b[:, ic * 256 + oc * 128: ic * 256 + (oc + 1) * 128],
                            h2b[:, ic * 512:(ic + 1) * 512], start=(ic == 0), stop=(ic == 3)),
                          reads=[r_cn3b, r_h2], writes=[rhh])
                    E("S", lambda e, oc=oc, ph=ph: e.activation(
                        h3b[:, oc * 512:(oc + 1) * 512], ph[:], AF.Gelu, bias=biasb[:, 12 + oc:13 + oc]),
                      reads=[rhh, r_biasb], writes=[r_h3])
                    yield
                for j in range(4):
                    for ic in range(2):
                        E("T", lambda e, j=j, ic=ic: e.matmul(
                            p[4][:, j:j + 1], h3b[:, ic * 512 + j * 128: ic * 512 + (j + 1) * 128],
                            cn4b[:, ic:ic + 1], start=(ic == 0), stop=(ic == 1)),
                          reads=[r_h3, r_cn4b], writes=[r_p[4]])
                drain_b0(999)
                E("V", lambda e, tc=tc: e.tensor_copy(tfb[:, tc * 4: tc * 4 + 4], p[4][:, 0:4]),
                  reads=[r_p[4]], writes=[r_tfb])
                E("S", lambda e, tc=tc: e.activation(
                    tfsg[:, tc * 4: tc * 4 + 4], tfb[:, tc * 4: tc * 4 + 4], AF.Sigmoid,
                    bias=scalB[:, 0:1]),
                  reads=[r_tfb, r_scalB], writes=[r_tfsg])
                E("V", lambda e, tc=tc: e.tensor_scalar(
                    coup[:, tc * 4: tc * 4 + 4], tfsg[:, tc * 4: tc * 4 + 4],
                    abb[:, 1:2], abb[:, 0:1], ALU.mult, ALU.add),
                  reads=[r_tfsg, r_abb], writes=[r_coup])
                E("V", lambda e, tc=tc: e.tensor_scalar(
                    ncoup[:, tc * 4: tc * 4 + 4], coup[:, tc * 4: tc * 4 + 4], -1.0, None, ALU.mult),
                  reads=[r_coup], writes=[r_ncoup])
                yield
                def tw_op(step, b, tt):
                    if step == 0:
                        return lambda: E("V", lambda e, b=b, tt=tt: e.tensor_scalar(
                            tmpq[:, (tt % 2) * 256:(tt % 2) * 256 + 256],
                            state[:, b + 512:b + 768], coup[:, tt:tt + 1], None, ALU.mult),
                          reads=[r_state[tt], r_coup], writes=[r_tmpq[tt % 2]])
                    elif step == 1:
                        return lambda: E("V", lambda e, b=b, tt=tt: e.scalar_tensor_tensor(
                            state[:, b + 512:b + 768], state[:, b + 0:b + 256], ncoup[:, tt:tt + 1],
                            state[:, b + 512:b + 768], ALU.mult, ALU.add),
                          reads=[r_state[tt], r_ncoup], writes=[r_state[tt]])
                    elif step == 2:
                        return lambda: E("V", lambda e, b=b, tt=tt: e.tensor_add(
                            state[:, b + 0:b + 256], state[:, b + 0:b + 256],
                            tmpq[:, (tt % 2) * 256:(tt % 2) * 256 + 256]),
                          reads=[r_tmpq[tt % 2], r_state[tt]], writes=[r_state[tt]])
                    elif step == 3:
                        return lambda: E("V", lambda e, b=b, tt=tt: e.tensor_scalar(
                            tmpq[:, (tt % 2) * 256:(tt % 2) * 256 + 256],
                            state[:, b + 768:b + 1024], coup[:, tt:tt + 1], None, ALU.mult),
                          reads=[r_state[tt], r_coup], writes=[r_tmpq[tt % 2]])
                    elif step == 4:
                        return lambda: E("V", lambda e, b=b, tt=tt: e.scalar_tensor_tensor(
                            state[:, b + 768:b + 1024], state[:, b + 256:b + 512], coup[:, tt:tt + 1],
                            state[:, b + 768:b + 1024], ALU.mult, ALU.add),
                          reads=[r_state[tt], r_coup], writes=[r_state[tt]])
                    else:
                        return lambda: E("V", lambda e, b=b, tt=tt: e.tensor_sub(
                            state[:, b + 256:b + 512], state[:, b + 256:b + 512],
                            tmpq[:, (tt % 2) * 256:(tt % 2) * 256 + 256]),
                          reads=[r_tmpq[tt % 2], r_state[tt]], writes=[r_state[tt]])
                for tp in (tc * 4, tc * 4 + 2):
                    for step in range(6):
                        for tt in (tp, tp + 1):
                            pend_tw.append(tw_op(step, tt * D, tt))

        units = [(c, t) for c in range(3) for t in range(nqc)]
        look = nqc > 1
        pend_tw = []

        def drain_tw(n):
            for _ in range(n):
                if pend_tw:
                    pend_tw.pop(0)()

        for _ in gen_ffn_A(*units[0]):
            drain_b0(2)
        for ui, (c, t) in enumerate(units):
            gb = gen_ffn_B(c, t)
            ga = gen_ffn_A(*units[ui + 1]) if look and ui + 1 < len(units) else None
            if nqc == 2:
                drain_tw(99)
            for _ in gb:
                if ga is not None:
                    next(ga, None)
                drain_tw(2)
                drain_b0(3)
            if ga is not None:
                for _ in ga:
                    pass
            if not look and ui + 1 < len(units):
                drain_tw(99)
                for _ in gen_ffn_A(*units[ui + 1]):
                    pass
        drain_tw(99)

        for tt in range(ntt):
            xb = tt % 2
            E("D", lambda e, tt=tt, xb=xb: e.dma_start(
                xload[:, xb * D:(xb + 1) * D], d_x[tt * 128:(tt + 1) * 128, :]),
              writes=[r_xload[xb]])
            E("D", lambda e, tt=tt, xb=xb: e.dma_start(
                sbtm[:, 0:D].bitcast(F32) if False else wstage3[:, xb * D:(xb + 1) * D],
                d_attn[tt * 128:(tt + 1) * 128, :]),
              reads=[r_attnd[tt]], writes=[r_wstage3[xb]])
            E("V", lambda e, tt=tt, xb=xb: e.tensor_add(
                state[:, tt * D:(tt + 1) * D], state[:, tt * D:(tt + 1) * D],
                xload[:, xb * D:(xb + 1) * D]),
              reads=[r_xload[xb], r_state[tt]], writes=[r_state[tt]])
            E("V", lambda e, tt=tt, xb=xb: e.tensor_add(
                state[:, tt * D:(tt + 1) * D], state[:, tt * D:(tt + 1) * D],
                wstage3[:, xb * D:(xb + 1) * D]),
              reads=[r_wstage3[xb], r_state[tt]], writes=[r_state[tt]])
            E("D", lambda e, tt=tt: e.dma_start(
                d_y[tt * 128:(tt + 1) * 128, :], state[:, tt * D:(tt + 1) * D]),
              reads=[r_state[tt]], writes=[r_y[tt]])

    from contextlib import ExitStack
    with ExitStack() as es:
        for ename in ctx.sem_names():
            ctx.sems[ename] = es.enter_context(nc.semaphore(f"sem_{ename}"))
        block = es.enter_context(nc.Block())

        ctx.begin("dry")
        script()
        final_snap = ctx._snap()

        def run_for(eng):
            def body(proxy):
                ctx.begin("emit", eng, proxy)
                script()
                if eng == "D":
                    for i in range(ctx.NDMA):
                        if final_snap[f"D{i}"] > 0:
                            proxy.wait_ge(ctx.sems[f"D{i}"], final_snap[f"D{i}"])
            return body

        block.tensor(run_for("T"))
        block.vector(run_for("V"))
        block.scalar(run_for("S"))
        block.sync(run_for("D"))
        block.gpsimd(run_for("G"))

    for n in list(cms)[::-1]:
        free(n)
    for cm in pcms[::-1]:
        cm.__exit__(None, None, None)
    return nc


_PROG_CACHE = {}


def _get_prog(T):
    if T not in _PROG_CACHE:
        _PROG_CACHE[T] = build_program(T)
    return _PROG_CACHE[T]


def make_in_maps(T, x_full, W):
    cos_s, sin_s, masks_bf, maskS, idn, idnb = host_constants(T)
    biasp = pack_biases(W["cn_b1"], W["cn_b2"], W["cn_b3"], W["gc_b1"], W["gc_b2"])
    scal3 = np.array([[np.asarray(W["cn_b4"]).reshape(-1)[0],
                       np.asarray(W["gc_b3"]).reshape(-1)[0],
                       np.asarray(W["ar"]).reshape(-1)[0]]], dtype=np.float32)
    cn4p = np.ascontiguousarray(np.asarray(W["cn_w4"], dtype=np.float32).reshape(2, 128).T)
    gc3p = np.ascontiguousarray(np.asarray(W["gc_w3"], dtype=np.float32).reshape(2, 128).T)
    shared = {
        "wqkv": np.ascontiguousarray(np.asarray(W["wqkv"], dtype=np.float32)),
        "wout": np.ascontiguousarray(np.asarray(W["wout"], dtype=np.float32)),
        "cn1": np.ascontiguousarray(np.asarray(W["cn_w1"], dtype=np.float32)),
        "cn2": np.ascontiguousarray(np.asarray(W["cn_w2"], dtype=np.float32)),
        "cn3": np.ascontiguousarray(np.asarray(W["cn_w3"], dtype=np.float32)),
        "cn4p": cn4p,
        "gc1": np.ascontiguousarray(np.asarray(W["gc_w1"], dtype=np.float32)),
        "gc2": np.ascontiguousarray(np.asarray(W["gc_w2"], dtype=np.float32)),
        "gc3p": gc3p,
        "biasp": biasp,
        "scal3": scal3,
        "cosb": cos_s,
        "sinb": sin_s,
        "maskp": masks_bf,
        "masksq": maskS,
        "idn": idn,
        "idnb": idnb,
    }
    maps = []
    for b in range(x_full.shape[0]):
        m = dict(shared)
        m["x"] = np.ascontiguousarray(np.asarray(x_full[b], dtype=np.float32))
        maps.append(m)
    return maps


def kernel(L0_attn_x, L0_attn_W_qkv, L0_attn_W_out,
           L0_ffn_cn_w1, L0_ffn_cn_b1, L0_ffn_cn_w2, L0_ffn_cn_b2,
           L0_ffn_cn_w3, L0_ffn_cn_b3, L0_ffn_cn_w4, L0_ffn_cn_b4,
           L0_ffn_gc_w1, L0_ffn_gc_b1, L0_ffn_gc_w2, L0_ffn_gc_b2,
           L0_ffn_gc_w3, L0_ffn_gc_b3, L0_ffn_adaptive_range):
    x = np.asarray(L0_attn_x, dtype=np.float32)
    B, T, _ = x.shape
    W = {
        "wqkv": L0_attn_W_qkv, "wout": L0_attn_W_out,
        "cn_w1": L0_ffn_cn_w1, "cn_b1": L0_ffn_cn_b1,
        "cn_w2": L0_ffn_cn_w2, "cn_b2": L0_ffn_cn_b2,
        "cn_w3": L0_ffn_cn_w3, "cn_b3": L0_ffn_cn_b3,
        "cn_w4": L0_ffn_cn_w4, "cn_b4": L0_ffn_cn_b4,
        "gc_w1": L0_ffn_gc_w1, "gc_b1": L0_ffn_gc_b1,
        "gc_w2": L0_ffn_gc_w2, "gc_b2": L0_ffn_gc_b2,
        "gc_w3": L0_ffn_gc_w3, "gc_b3": L0_ffn_gc_b3,
        "ar": L0_ffn_adaptive_range,
    }
    nc = _get_prog(T)
    in_maps = make_in_maps(T, x, W)
    res = run_bass_kernel_spmd(nc, in_maps, core_ids=list(range(B)))
    out = np.stack([np.asarray(res.results[b]["y"], dtype=np.float32) for b in range(B)], axis=0)
    return out


# version 40 (final)
# speedup vs baseline: 1.0000x; 1.0000x over previous
import numpy as np
import ml_dtypes
import concourse.bass as bass
import concourse.mybir as mybir
from concourse.bass_utils import run_bass_kernel_spmd

F32 = mybir.dt.float32
BF16 = mybir.dt.bfloat16
AF = mybir.ActivationFunctionType
ALU = mybir.AluOpType
AX = mybir.AxisListType

D = 1024
NH = 16
HD = 64
NIC = D // 128

RELAX_ENGINES = {"T", "S"}
DRAIN_NOW = False


_ALL_RES = []


class Res:
    __slots__ = ("last_w", "readers")

    def __init__(self):
        self.last_w = None
        self.readers = []
        _ALL_RES.append(self)


class Ctx:
    ENGINES = ("T", "V", "S", "D", "G")
    NDMA = 8

    def __init__(self):
        self.tokens = []
        self.counts = {e: 0 for e in ("T", "V", "S", "G")}
        self.dma_i = 0
        self.sems = {}
        self.mode = "dry"
        self.emit_eng = None
        self.proxy = None
        self.idx = 0
        self.watermark = {}

    def sem_names(self):
        return ["T", "V", "S", "G"] + [f"D{i}" for i in range(self.NDMA)]

    def begin(self, mode, eng=None, proxy=None):
        self.mode = mode
        self.emit_eng = eng
        self.proxy = proxy
        self.idx = 0
        self.dma_i = 0
        self.counts = {e: 0 for e in ("T", "V", "S", "G")}
        self.watermark = {s: 0 for s in self.sem_names()}
        for r in _ALL_RES:
            r.last_w = None
            r.readers = []

    def _wait(self, sem, val):
        if val <= 0 or self.watermark[sem] >= val:
            return
        self.watermark[sem] = val
        self.proxy.wait_ge(self.sems[sem], val)

    def emit(self, eng, fn, reads=(), writes=()):
        i = self.idx
        self.idx += 1
        if eng == "D":
            di = self.dma_i
            self.dma_i += 1
            pool = f"D{di % self.NDMA}"
            tok = (pool, 16 * (di // self.NDMA + 1))
        else:
            if self.mode == "dry":
                self.counts[eng] += 1
            tok = (eng, (self.counts[eng] if self.mode == "dry" else None))
        if self.mode == "dry":
            if eng == "D":
                self.tokens.append(tok)
            else:
                self.tokens.append(tok)
        else:
            tok = self.tokens[i]
            if eng == self.emit_eng:
                deps = []
                for r in reads:
                    if r.last_w is not None:
                        deps.append(r.last_w)
                for w in writes:
                    if w.last_w is not None:
                        deps.append(w.last_w)
                    deps.extend(w.readers)
                if eng == "D":
                    deps.append((tok[0], tok[1] - 16))
                for de, dv in deps:
                    if de == eng and dv >= tok[1]:
                        continue
                    if de == eng and eng in RELAX_ENGINES:
                        continue
                    self._wait(de, dv)
                instr = fn(self.proxy)
                instr.then_inc(self.sems[tok[0]], 16 if eng == "D" else 1)
        for r in reads:
            r.readers.append(tok)
        for w in writes:
            w.last_w = tok
            w.readers = []
        return tok

    def _snap(self):
        s = {e: self.counts[e] for e in ("T", "V", "S", "G")}
        for i in range(self.NDMA):
            n = self.dma_i
            s[f"D{i}"] = 16 * ((n - i + self.NDMA - 1) // self.NDMA)
        return s

    def barrier(self):
        i = self.idx
        self.idx += 1
        if self.mode == "dry":
            self.tokens.append(("BARRIER", self._snap()))
            return
        snap = self.tokens[i][1]
        for sname, v in snap.items():
            if sname != self.emit_eng:
                self._wait(sname, v)


def host_constants(T):
    ntt = T // 128
    freqs = (1.0 / (np.float32(10000.0) ** (np.arange(0, HD, 2, dtype=np.float32) / np.float32(HD)))).astype(np.float32)
    t = np.arange(T, dtype=np.float32)
    ang = np.outer(t, freqs).astype(np.float32)
    cos = np.cos(ang).astype(np.float32)
    sin = np.sin(ang).astype(np.float32)
    cos_s = np.ascontiguousarray(cos.reshape(ntt, 128, 32).transpose(1, 0, 2).reshape(128, ntt * 32))
    sin_s = np.ascontiguousarray(sin.reshape(ntt, 128, 32).transpose(1, 0, 2).reshape(128, ntt * 32))
    masks = np.zeros((128, 4 * 512), dtype=np.float32)
    kk = np.arange(128)[:, None]
    qq = np.arange(512)[None, :]
    for m in range(4):
        valid = (qq >= 128 * m + kk) | (qq >= 128 * (m + 1))
        dead = qq < 128 * m
        masks[:, m * 512:(m + 1) * 512] = np.where(dead, 0.0, np.where(valid, 1.0, 0.0))
    masks_bf = masks.astype(ml_dtypes.bfloat16)
    maskS = np.where(np.arange(128)[None, :] > np.arange(128)[:, None], -30000.0, 0.0).astype(np.float32)
    idn = np.eye(128, dtype=np.float32)
    idnb = idn.astype(ml_dtypes.bfloat16)
    return cos_s, sin_s, masks_bf, maskS, idn, idnb


def pack_biases(cn_b1, cn_b2, cn_b3, gc_b1, gc_b2):
    out = np.zeros((128, 20), dtype=np.float32)
    out[:, 0:8] = np.asarray(cn_b1).reshape(8, 128).T
    out[:, 8:12] = np.asarray(cn_b2).reshape(4, 128).T
    out[:, 12:14] = np.asarray(cn_b3).reshape(2, 128).T
    out[:, 14:18] = np.asarray(gc_b1).reshape(4, 128).T
    out[:, 18:20] = np.asarray(gc_b2).reshape(2, 128).T
    return out


def build_program(T):
    _ALL_RES.clear()
    ntt = T // 128
    nqc = T // 512
    nkt = T // 128

    nc = bass.Bass("TRN2", target_bir_lowering=False,
                   detect_race_conditions=(not RELAX_ENGINES))

    d_x = nc.dram_tensor("x", [T, D], F32, kind="ExternalInput")
    d_wqkv = nc.dram_tensor("wqkv", [3 * D, D], F32, kind="ExternalInput")
    d_wout = nc.dram_tensor("wout", [D, D], F32, kind="ExternalInput")
    d_cn1 = nc.dram_tensor("cn1", [D, D], F32, kind="ExternalInput")
    d_cn2 = nc.dram_tensor("cn2", [D, 512], F32, kind="ExternalInput")
    d_cn3 = nc.dram_tensor("cn3", [512, 256], F32, kind="ExternalInput")
    d_cn4 = nc.dram_tensor("cn4p", [128, 2], F32, kind="ExternalInput")
    d_gc1 = nc.dram_tensor("gc1", [D, 512], F32, kind="ExternalInput")
    d_gc2 = nc.dram_tensor("gc2", [512, 256], F32, kind="ExternalInput")
    d_gc3 = nc.dram_tensor("gc3p", [128, 2], F32, kind="ExternalInput")
    d_biasp = nc.dram_tensor("biasp", [128, 20], F32, kind="ExternalInput")
    d_scal3 = nc.dram_tensor("scal3", [1, 3], F32, kind="ExternalInput")
    d_cos = nc.dram_tensor("cosb", [128, ntt * 32], F32, kind="ExternalInput")
    d_sin = nc.dram_tensor("sinb", [128, ntt * 32], F32, kind="ExternalInput")
    d_mask = nc.dram_tensor("maskp", [128, 4 * 512], BF16, kind="ExternalInput")
    d_masksq = nc.dram_tensor("masksq", [128, 128], F32, kind="ExternalInput")
    d_idn = nc.dram_tensor("idn", [128, 128], F32, kind="ExternalInput")
    d_idnb = nc.dram_tensor("idnb", [128, 128], BF16, kind="ExternalInput")
    d_attn = nc.dram_tensor("attn_scr", [T, D], F32, kind="Internal")
    d_rbc = nc.dram_tensor("rbc_scr", [2, 512], F32, kind="Internal")
    d_y = nc.dram_tensor("y", [T, D], F32, kind="ExternalOutput")

    cms = {}

    def alloc(name, shape, dtype):
        cm = nc.sbuf_tensor(name, shape, dtype)
        h = cm.__enter__()
        cms[name] = cm
        return h

    def free(*names):
        for n in names:
            cms.pop(n).__exit__(None, None, None)

    xload = alloc("xload", [128, 2 * D], F32)
    sums = alloc("sums", [128, ntt], F32)
    ssq = alloc("ssq", [128, ntt], F32)
    mcol = alloc("mcol", [128, ntt], F32)
    rstd = alloc("rstd", [128, ntt], F32)
    vtmp = alloc("vtmp", [128, 8], F32)
    epsb = alloc("epsb", [128, 1], F32)
    biasb = alloc("biasb", [128, 20], F32)
    scal3b = alloc("scal3b", [1, 3], F32)
    scalB = alloc("scalB", [128, 3], F32)
    ones1 = alloc("ones1", [1, 128], F32)
    gmean = alloc("gmean", [128, NIC], F32)
    gmeanb = alloc("gmeanb", [128, NIC], BF16)
    idn = alloc("idn_s", [128, 128], F32)
    idnb = alloc("idnb_s", [128, 128], BF16)
    qkm = alloc("qkm", [128, 64], F32)
    redqk = alloc("redqk", [64, 1], F32)
    crow = alloc("crow", [1, 64], F32)
    negc = alloc("negc", [128, NH], F32)
    afst = alloc("afst", [128, 2 * 512], F32)

    q_fm = alloc("q_fm", [128, NIC * T], BF16)
    k_fm = alloc("k_fm", [128, NIC * T], BF16)
    v_ag = alloc("v_ag", [128, ntt * (NH * 65)], BF16)
    maskb = alloc("maskb", [128, 4 * 512], BF16)
    otmp = alloc("otmp", [64, 2 * 512], BF16)
    recips = alloc("recips", [1, 2 * 512], F32)
    rbcast = alloc("rbcast", [64, 2 * 512], F32)

    scr = alloc("scr", [128, D], F32)
    scr2 = alloc("scr2x", [128, 64], F32)
    cosb = alloc("cosb_s", [128, ntt * 32], F32)
    sinb = alloc("sinb_s", [128, ntt * 32], F32)
    xn_tm = alloc("xn_tm", [128, D], BF16)
    xn_fm = alloc("xn_fm", [128, NIC * T], BF16)
    wstage = alloc("wstage", [128, D], F32)
    wcast = alloc("wcast", [128, D], BF16)
    wT = alloc("wT", [128, 2 * NIC * 512], BF16)
    ropetm = alloc("ropetm", [128, 2 * 512], BF16)
    qss = alloc("qss", [128, NH * ntt], F32)
    kss = alloc("kss", [128, NH * ntt], F32)

    free("kss", "qss", "ropetm", "wT", "wcast", "wstage", "xn_fm", "xn_tm",
         "sinb_s", "cosb_s", "scr2x", "scr")
    ptb = alloc("ptb", [128, nkt * 512], BF16)
    kaug = alloc("kaug", [65, 2 * nkt * 128], BF16)
    qaug = alloc("qaug", [65, 8 * 512], BF16)
    dumpW = alloc("dumpW", [128, 2 * 512], BF16)
    sumch = alloc("sumch", [128, 128], F32)
    sumq = alloc("sumq", [128, NH * ntt], F32)
    cqb = alloc("cqb", [128, NH * ntt], F32)
    maskSb = alloc("maskSb", [128, 128], F32)
    negc8 = alloc("negc8", [128, NH], F32)
    negcd8 = alloc("negcd8", [128, NH], F32)
    woutT = alloc("woutT", [128, NIC * D], BF16)
    wstage2 = alloc("wstage2", [128, 2 * D], F32)
    wcast2 = alloc("wcast2", [128, D], BF16)
    astage = alloc("astage", [128, 4 * D], F32)

    free("astage", "wcast2", "wstage2", "woutT",
         "negcd8", "negc8", "maskSb", "cqb", "sumq", "sumch", "dumpW", "qaug", "kaug", "ptb",
         "rbcast", "recips", "otmp", "maskb", "v_ag", "k_fm", "q_fm")
    state = alloc("state", [128, ntt * D], F32)
    sbtm = alloc("sbtm", [128, 2 * D], BF16)
    sfm = alloc("sfm", [128, 2 * NIC * 512], BF16)
    h1b = alloc("h1b", [128, 8 * 512], BF16)
    h2b = alloc("h2b", [128, 4 * 512], BF16)
    h3b = alloc("h3b", [128, 2 * 512], BF16)
    tfb = alloc("tfb", [128, ntt], F32)
    tfsg = alloc("tfsg", [128, ntt], F32)
    coup = alloc("coup", [128, ntt], F32)
    ncoup = alloc("ncoup", [128, ntt], F32)
    tmpq = alloc("tmpq", [128, 512], F32)
    gb1 = alloc("gb1", [128, 4], BF16)
    gb2 = alloc("gb2", [128, 2], BF16)
    ab2 = alloc("ab2", [1, 8], F32)
    abb = alloc("abb", [128, 2], F32)
    cn1b = alloc("cn1b", [128, 8 * D], BF16)
    cn2b = alloc("cn2b", [128, 8 * 512], BF16)
    cn3b = alloc("cn3b", [128, 4 * 256], BF16)
    cn4b = alloc("cn4b", [128, 2], BF16)
    gc1b = alloc("gc1b", [128, 8 * 512], BF16)
    gc2b = alloc("gc2b", [128, 4 * 256], BF16)
    gc3b = alloc("gc3b", [128, 2], BF16)
    wstage3 = alloc("wstage3", [128, 2 * D], F32)

    pcms = []

    def palloc(name, shape, dtype):
        cm = nc.psum_tensor(name, shape, dtype)
        h = cm.__enter__()
        pcms.append(cm)
        return h

    pp = [palloc(f"pp{i}", [128, 1024], F32) for i in range(3)]
    p6t = palloc("p6", [128, 1024], BF16)
    p7t = palloc("p7", [128, 1024], BF16)

    class PV:
        def __init__(self, t, base):
            self.t = t
            self.base = base

        def __getitem__(self, key):
            if not isinstance(key, tuple):
                return self.t[:, self.base:self.base + 512]
            rows, cols = key
            if cols == slice(None):
                cols = slice(0, 512)
            return self.t[rows, self.base + (cols.start or 0): self.base + cols.stop]

    pv8 = [PV(pp[0], 0), PV(pp[0], 512), PV(pp[1], 0), PV(pp[1], 512),
           PV(pp[2], 0), PV(pp[2], 512)]
    p = pv8 + [p6t, p7t]

    r_xload = [Res(), Res()]
    r_sums = [Res(), Res()]
    r_ssq = [Res(), Res()]
    r_mcol = [Res(), Res()]
    r_rstd = [Res(), Res()]
    r_vtmp = [Res(), Res()]
    r_epsb = Res()
    r_cos, r_sin, r_biasb, r_scal3, r_scalB, r_ones1 = (Res() for _ in range(6))
    r_gmean, r_gmeanb, r_idn, r_idnb, r_scr, r_scr2 = (Res() for _ in range(6))
    r_qkm, r_redqk, r_crow, r_negc = (Res() for _ in range(4))
    r_xn_tm = Res()
    r_xn_fm = Res()
    r_wstage = Res()
    r_wcast = Res()
    r_wT = [Res(), Res()]
    r_ropetm = [Res(), Res()]
    r_qss, r_kss = Res(), Res()
    r_qh = [[Res() for _ in range(T // 512)] for _ in range(NH)]
    r_kh = [Res() for _ in range(NH)]
    r_vag = Res()
    r_pt = [Res() for _ in range(nkt)]
    r_maskb = Res()
    r_recips = [Res(), Res()]
    r_rbc_d = [Res(), Res()]
    r_rbcast = [Res(), Res()]
    r_otmp = [Res(), Res()]
    r_p = [Res() for _ in range(8)]
    r_woutT, r_wcast2 = Res(), Res()
    r_kaug = [Res(), Res()]
    r_qaug = [Res() for _ in range(8)]
    r_dumpW = [Res(), Res()]
    r_sumch = [[Res() for _ in range(16)], [Res() for _ in range(16)]]
    r_sumq, r_maskSb = Res(), Res()
    r_cqb = [Res() for _ in range(NH)]
    r_negc8, r_negcd8 = Res(), Res()
    r_wstage2 = [Res(), Res()]
    r_afst = [Res(), Res()]
    r_astage = Res()
    r_attnd = [Res() for _ in range(ntt)]
    r_state = [Res() for _ in range(ntt)]
    r_sbtm = Res()
    r_sfm = [Res(), Res()]
    r_h1, r_h2, r_h3 = (Res() for _ in range(3))
    r_tfb, r_tfsg, r_coup, r_ncoup = (Res() for _ in range(4))
    r_tmpq = [Res(), Res()]
    r_gb1, r_gb2, r_ab2, r_abb = (Res() for _ in range(4))
    r_cn1b, r_cn2b, r_cn3b, r_cn4b = (Res() for _ in range(4))
    r_gc1b, r_gc2b, r_gc3b = (Res() for _ in range(3))
    r_wstage3 = [Res(), Res()]
    r_y = [Res() for _ in range(ntt)]

    ctx = Ctx()

    def script():
        E = ctx.emit

        E("D", lambda e: e.dma_start(cosb[:], d_cos[:]), writes=[r_cos])
        E("D", lambda e: e.dma_start(sinb[:], d_sin[:]), writes=[r_sin])
        E("D", lambda e: e.dma_start(biasb[:], d_biasp[:]), writes=[r_biasb])
        E("D", lambda e: e.dma_start(scal3b[:], d_scal3[:]), writes=[r_scal3])
        E("D", lambda e: e.dma_start(maskb[:], d_mask[:]), writes=[r_maskb])
        E("D", lambda e: e.dma_start(idn[:], d_idn[:]), writes=[r_idn])
        E("D", lambda e: e.dma_start(idnb[:], d_idnb[:]), writes=[r_idnb])
        E("V", lambda e: e.memset(epsb[:], 1e-5), writes=[r_epsb])
        E("V", lambda e: e.memset(ones1[:], 1.0), writes=[r_ones1])
        for tt in range(ntt):
            E("V", lambda e, tt=tt: e.memset(
                bass.AP(v_ag, tt * (NH * 65) + 64, [[ntt * NH * 65, 128], [65, NH], [1, 1]]), 1.0),
              writes=[r_vag])
        E("T", lambda e: e.matmul(p[5][0:128, 0:3], ones1[:], scal3b[:], start=True, stop=True),
          reads=[r_ones1, r_scal3], writes=[r_p[5]])
        E("V", lambda e: e.tensor_copy(scalB[:], p[5][0:128, 0:3]), reads=[r_p[5]], writes=[r_scalB])

        def ln_tile(tt):
            xb = tt % 2
            E("D", lambda e, tt=tt, xb=xb: e.dma_start(
                xload[:, xb * D:(xb + 1) * D], d_x[tt * 128:(tt + 1) * 128, :]), writes=[r_xload[xb]])
            v0 = xb * 4
            E("V", lambda e, tt=tt, xb=xb: e.reduce_sum(
                sums[:, tt:tt + 1], xload[:, xb * D:(xb + 1) * D], axis=AX.X),
              reads=[r_xload[xb]], writes=[r_sums[xb]])
            E("S", lambda e, tt=tt, xb=xb: e.activation(
                afst[:, 0:D], xload[:, xb * D:(xb + 1) * D], AF.Square, accum_out=ssq[:, tt:tt + 1]),
              reads=[r_xload[xb]], writes=[r_afst[0], r_afst[1], r_ssq[xb]])
            E("V", lambda e, tt=tt, xb=xb: e.tensor_scalar(
                mcol[:, tt:tt + 1], sums[:, tt:tt + 1], 1.0 / D, None, ALU.mult),
              reads=[r_sums[xb]], writes=[r_mcol[xb]])
            E("V", lambda e, tt=tt, v0=v0, xb=xb: e.tensor_mul(
                vtmp[:, v0:v0 + 1], mcol[:, tt:tt + 1], mcol[:, tt:tt + 1]),
              reads=[r_mcol[xb]], writes=[r_vtmp[xb]])
            E("V", lambda e, tt=tt, v0=v0, xb=xb: e.tensor_scalar(
                vtmp[:, v0 + 1:v0 + 2], ssq[:, tt:tt + 1], 1.0 / D, None, ALU.mult),
              reads=[r_ssq[xb]], writes=[r_vtmp[xb]])
            E("V", lambda e, v0=v0, xb=xb: e.tensor_sub(
                vtmp[:, v0 + 2:v0 + 3], vtmp[:, v0 + 1:v0 + 2], vtmp[:, v0:v0 + 1]),
              reads=[r_vtmp[xb]], writes=[r_vtmp[xb]])
            E("S", lambda e, v0=v0, xb=xb: e.activation(
                vtmp[:, v0 + 3:v0 + 4], vtmp[:, v0 + 2:v0 + 3], AF.Sqrt, bias=epsb[:]),
              reads=[r_vtmp[xb], r_epsb], writes=[r_vtmp[xb]])
            E("V", lambda e, tt=tt, v0=v0, xb=xb: e.reciprocal(
                rstd[:, tt:tt + 1], vtmp[:, v0 + 3:v0 + 4]),
              reads=[r_vtmp[xb]], writes=[r_rstd[xb]])
            E("V", lambda e, tt=tt, xb=xb: e.tensor_scalar(
                xn_tm[:], xload[:, xb * D:(xb + 1) * D],
                mcol[:, tt:tt + 1], rstd[:, tt:tt + 1], ALU.subtract, ALU.mult),
              reads=[r_xload[xb], r_mcol[xb], r_rstd[xb]], writes=[r_xn_tm])
            for half in range(2):
                pb, rb = p[6 + half], r_p[6 + half]
                for j in range(4):
                    ic = half * 4 + j
                    E("T", lambda e, ic=ic, j=j, pb=pb: e.transpose(
                        pb[:, j * 128:(j + 1) * 128], xn_tm[:, ic * 128:(ic + 1) * 128], idnb[:]),
                      reads=[r_xn_tm, r_idnb], writes=[rb])
                E("S", lambda e, tt=tt, half=half, pb=pb: e.copy(
                    bass.AP(xn_fm, half * 4 * T + tt * 128, [[NIC * T, 128], [T, 4], [1, 128]]),
                    bass.AP(pb, 0, [[1024, 128], [128, 4], [1, 128]])),
                  reads=[rb], writes=[r_xn_fm])

        for tt in range(min(6, ntt)):
            ln_tile(tt)
        lnq = list(range(6, ntt))

        def emit_gmean():
            for ic in range(NIC):
                E("V", lambda e, ic=ic: e.reduce_sum(
                    gmean[:, ic:ic + 1], xn_fm[:, ic * T:(ic + 1) * T], axis=AX.X),
                  reads=[r_xn_fm], writes=[r_gmean])
            E("V", lambda e: e.tensor_scalar(gmean[:], gmean[:], 1.0 / T, None, ALU.mult),
              reads=[r_gmean], writes=[r_gmean])
            E("V", lambda e: e.tensor_copy(gmeanb[:], gmean[:]), reads=[r_gmean], writes=[r_gmeanb])

        def build_wT(oc):
            ops = []
            wb = oc % 2
            for os_ in range(4):
                ops.append(lambda oc=oc, os_=os_: E("D", lambda e, oc=oc, os_=os_: e.dma_start(
                    wstage[:], d_wqkv[oc * 512 + os_ * 128: oc * 512 + (os_ + 1) * 128, :]),
                  writes=[r_wstage]))
                ops.append(lambda: E("V", lambda e: e.tensor_copy(wcast[:], wstage[:]),
                  reads=[r_wstage], writes=[r_wcast]))
                for half in range(2):
                    pb, rb = p[6 + half], r_p[6 + half]

                    def tr_ev(half=half, os_=os_, pb=pb, rb=rb, wb=wb):
                        for j in range(4):
                            ic = half * 4 + j
                            E("T", lambda e, ic=ic, j=j, pb=pb: e.transpose(
                                pb[:, j * 128:(j + 1) * 128], wcast[:, ic * 128:(ic + 1) * 128], idnb[:]),
                              reads=[r_wcast, r_idnb], writes=[rb])
                        E("S", lambda e, wb=wb, half=half, os_=os_, pb=pb: e.copy(
                            bass.AP(wT, wb * NIC * 512 + half * 4 * 512 + os_ * 128,
                                    [[2 * NIC * 512, 128], [512, 4], [1, 128]]),
                            bass.AP(pb, 0, [[1024, 128], [128, 4], [1, 128]])),
                          reads=[rb], writes=[r_wT[wb]])
                    ops.append(tr_ev)
            return ops

        for op in build_wT(0):
            op()
        for oc in range(6):
            wb = oc % 2
            wbq = build_wT(oc + 1) if oc < 5 else []
            pend = [None]

            def flush_pend():
                if pend[0] is not None:
                    for f in pend[0]:
                        f()
                    pend[0] = None

            for tt in range(ntt):
                pq, rq = p[tt % 4], r_p[tt % 4]
                for ic in range(NIC):
                    E("T", lambda e, tt=tt, ic=ic, wb=wb, pq=pq: e.matmul(
                        pq[:], xn_fm[:, ic * T + tt * 128: ic * T + (tt + 1) * 128],
                        wT[:, wb * NIC * 512 + ic * 512: wb * NIC * 512 + (ic + 1) * 512],
                        start=(ic == 0), stop=(ic == NIC - 1)),
                      reads=[r_xn_fm, r_wT[wb]], writes=[rq])
                flush_pend()
                for _ in range(3):
                    if wbq:
                        wbq.pop(0)()
                if oc == 0 and lnq:
                    ln_tile(lnq.pop(0))
                if oc < 4:
                    rb_ = tt % 2

                    def ev(base, pq=pq):
                        return bass.AP(pq.t, pq.base + base, [[1024, 128], [64, 8], [2, 32]])

                    def evr(base, rb_=rb_):
                        return bass.AP(ropetm, rb_ * 512 + base, [[2 * 512, 128], [64, 8], [2, 32]])

                    def cosap(tt=tt):
                        return bass.AP(cosb, tt * 32, [[ntt * 32, 128], [0, 8], [1, 32]])

                    def sinap(tt=tt):
                        return bass.AP(sinb, tt * 32, [[ntt * 32, 128], [0, 8], [1, 32]])

                    def scrap(k):
                        return bass.AP(scr, k * 256, [[D, 128], [32, 8], [1, 32]])

                    E("V", lambda e, ev=ev, cosap=cosap, scrap=scrap: e.tensor_mul(scrap(0), ev(0), cosap()),
                      reads=[rq, r_cos], writes=[r_scr])
                    E("V", lambda e, ev=ev, sinap=sinap, scrap=scrap: e.tensor_mul(scrap(1), ev(1), sinap()),
                      reads=[rq, r_sin], writes=[r_scr])
                    E("V", lambda e, ev=ev, sinap=sinap, scrap=scrap: e.tensor_mul(scrap(2), ev(0), sinap()),
                      reads=[rq, r_sin], writes=[r_scr])
                    E("V", lambda e, ev=ev, cosap=cosap, scrap=scrap: e.tensor_mul(scrap(3), ev(1), cosap()),
                      reads=[rq, r_cos], writes=[r_scr])
                    E("V", lambda e, evr=evr, scrap=scrap, rb_=rb_: e.tensor_sub(evr(0), scrap(0), scrap(1)),
                      reads=[r_scr], writes=[r_ropetm[rb_]])
                    E("V", lambda e, evr=evr, scrap=scrap, rb_=rb_: e.tensor_add(evr(1), scrap(2), scrap(3)),
                      reads=[r_scr], writes=[r_ropetm[rb_]])
                    def mk_pend(tt=tt, rb_=rb_, oc=oc):
                        fs = []
                        pb, rb2 = p[6 + (tt % 2)], r_p[6 + (tt % 2)]
                        for j in range(4):
                            def f(j=j, rb_=rb_, pb=pb, rb2=rb2):
                                E("T", lambda e, rb_=rb_, j=j, pb=pb: e.transpose(
                                    pb[:, j * 128:(j + 1) * 128],
                                    ropetm[:, rb_ * 512 + j * 128: rb_ * 512 + (j + 1) * 128], idnb[:]),
                                  reads=[r_ropetm[rb_], r_idnb], writes=[rb2])
                            fs.append(f)
                        dst = q_fm if oc < 2 else k_fm
                        if oc < 2:
                            rdst = [rr for j in range(4) for dj in range(2)
                                    for rr in r_qh[(oc % 2) * 8 + 2 * j + dj][tt // 4: tt // 4 + 1]]
                        else:
                            rdst = [r_kh[(oc % 2) * 8 + 2 * j + dj] for j in range(4) for dj in range(2)]
                        ch0 = (oc % 2) * 4

                        def fev(tt=tt, ch0=ch0, pb=pb, rb2=rb2, dst=dst, rdst=rdst):
                            E("S", lambda e, tt=tt, ch0=ch0, pb=pb, dst=dst: e.copy(
                                bass.AP(dst, ch0 * T + tt * 128, [[NIC * T, 128], [T, 4], [1, 128]]),
                                bass.AP(pb, 0, [[1024, 128], [128, 4], [1, 128]])),
                              reads=[rb2], writes=rdst)
                        fs.append(fev)
                        return fs

                    pend[0] = mk_pend()
                else:
                    h0 = (oc - 4) * 8
                    E("S", lambda e, tt=tt, h0=h0, pq=pq: e.copy(
                        bass.AP(v_ag, tt * (NH * 65) + h0 * 65, [[ntt * NH * 65, 128], [65, 8], [1, 64]]),
                        bass.AP(pq.t, pq.base, [[1024, 128], [64, 8], [1, 64]])),
                      reads=[rq], writes=[r_vag])
            flush_pend()
            for op in wbq:
                op()
            if oc == 0:
                while lnq:
                    ln_tile(lnq.pop(0))
                emit_gmean()

        ctx.barrier()
        E("D", lambda e: e.dma_start(maskSb[:], d_masksq[:]), writes=[r_maskSb])
        E("V", lambda e: e.memset(kaug[64:65, :], 1.0), writes=[r_kaug[0], r_kaug[1]])
        for hp2 in range(2):
            for qt in range(ntt):
                npair = ((qt * 128) // 512 + 1) // 2
                pads = [(1, -3.0e4)]
                if npair == 0 or (npair == 1 and qt % 2 == 1):
                    pads.append((2, -3.0e4))
                if npair == 0 or (npair == 1 and qt % 2 == 0):
                    pads.append((3, 1e-30))
                for slot, val in pads:
                    E("V", lambda e, hp2=hp2, qt=qt, slot=slot, val=val: e.memset(
                        sumch[:, hp2 * 64 + qt * 4 + slot: hp2 * 64 + qt * 4 + slot + 1], val),
                      writes=[r_sumch[hp2][qt]])

        wprep = []

        def wadd(eng, fn, rd, wr):
            wprep.append((eng, fn, rd, wr))

        for os_ in range(8):
            sb = os_ % 2
            wadd("D", lambda e, os_=os_, sb=sb: e.dma_start(
                wstage2[:, sb * D:(sb + 1) * D], d_wout[os_ * 128:(os_ + 1) * 128, :]),
                [], [r_wstage2[sb]])
            wadd("V", lambda e, sb=sb: e.tensor_copy(wcast2[:], wstage2[:, sb * D:(sb + 1) * D]),
                 [r_wstage2[sb]], [r_wcast2])
            for half in range(2):
                pb, rb = p[6 + half], r_p[6 + half]
                for j in range(4):
                    dc = half * 4 + j
                    wadd("T", lambda e, dc=dc, j=j, pb=pb: e.transpose(
                        pb[:, j * 128:(j + 1) * 128], wcast2[:, dc * 128:(dc + 1) * 128], idnb[:]),
                        [r_wcast2, r_idnb], [rb])
                wadd("V", lambda e, os_=os_, half=half, pb=pb: e.tensor_copy(
                    bass.AP(woutT, half * 4 * D + os_ * 128, [[NIC * D, 128], [D, 4], [1, 128]]),
                    bass.AP(pb, 0, [[1024, 128], [128, 4], [1, 128]])),
                    [rb], [r_woutT])
        wi = [0]

        def drip(n):
            for _ in range(n):
                if wi[0] < len(wprep):
                    eng, fn, rd, wr = wprep[wi[0]]
                    wi[0] += 1
                    E(eng, fn, reads=rd, writes=wr)

        tail1 = [None]
        tail2 = [None]

        def flush_tail1():
            if tail1[0] is not None:
                for f in tail1[0]:
                    f()
                tail1[0] = None

        def flush_tail():
            flush_tail1()
            if tail2[0] is not None:
                for f in tail2[0]:
                    f()
                tail2[0] = None

        pend_pv = []

        def _pop_pv():
            kt, fn, bd = pend_pv.pop(0)
            if bd:
                flush_tail()
            fn()

        def drain_pv(min_kt):
            while pend_pv and pend_pv[0][0] >= min_kt:
                _pop_pv()

        def drain_pv_n(n):
            for _ in range(n):
                if pend_pv:
                    _pop_pv()

        def gen_passA(h):
            ch = h // 2
            hb = (h % 2) * 64
            kslot = (h % 2) * nkt * 128
            hp = h % 2
            for kt in range(nkt):
                E("G", lambda e, kt=kt, ch=ch, hb=hb, kslot=kslot: e.tensor_copy(
                    kaug[0:64, kslot + kt * 128: kslot + (kt + 1) * 128],
                    k_fm[hb:hb + 64, ch * T + kt * 128: ch * T + (kt + 1) * 128]),
                  reads=[r_kh[h]], writes=[r_kaug[h % 2]])
            for qc in range(nqc):
                sl = (h % 2) * 4 + qc
                E("G", lambda e, qc=qc, ch=ch, hb=hb, sl=sl: e.tensor_copy(
                    qaug[0:64, sl * 512:(sl + 1) * 512],
                    q_fm[hb:hb + 64, ch * T + qc * 512: ch * T + (qc + 1) * 512]),
                  reads=[r_qh[h][qc]], writes=[r_qaug[sl]])
            yield
            for gi in range(ntt):
                qt = gi
                nfull = (qt * 128) // 512
                rem = qt * 128 - nfull * 512
                k0, kw = nfull * 512, rem + 128
                ps, rs = p[gi % 4], r_p[gi % 4]
                E("T", lambda e, qt=qt, k0=k0, kw=kw, ps=ps, ch=ch, hb=hb: e.matmul(
                    ps[:, 0:kw], q_fm[hb:hb + 64, ch * T + qt * 128: ch * T + (qt + 1) * 128],
                    k_fm[hb:hb + 64, ch * T + k0: ch * T + k0 + kw],
                    start=True, stop=True),
                  reads=[r_qh[h][qt // 4], r_kh[h]], writes=[rs])
                E("V", lambda e, ps=ps, rem=rem: e.tensor_add(
                    ps[:, rem:rem + 128], ps[:, rem:rem + 128], maskSb[:]),
                  reads=[r_maskSb, rs], writes=[rs])
                E("V", lambda e, ps=ps, kw=kw, qt=qt, hp=hp: e.reduce_max(
                    sumch[:, hp * 64 + qt * 4: hp * 64 + qt * 4 + 1],
                    ps[:, 0:kw], axis=AX.X),
                  reads=[rs], writes=[r_sumch[hp][qt]])
                if gi == 7:
                    flush_tail()
                if gi % 2 == 1:
                    if gi >= 9:
                        drain_pv_n(2)
                    yield
            pairi = 0
            for qt in range(ntt):
                nfull = (qt * 128) // 512
                npair = (nfull + 1) // 2
                fi = 0
                pq = 0
                while fi < nfull:
                    take = [(f * 512, 512) for f in (fi, fi + 1) if f < nfull]
                    fi += len(take)
                    pair = pairi % 2
                    pairi += 1
                    drain_pv_n(1)
                    halves = []
                    for hi, (k0, kw) in enumerate(take):
                        ps, rs = p[2 * pair + hi], r_p[2 * pair + hi]
                        E("T", lambda e, qt=qt, k0=k0, kw=kw, ps=ps, ch=ch, hb=hb: e.matmul(
                            ps[:, 0:kw], q_fm[hb:hb + 64, ch * T + qt * 128: ch * T + (qt + 1) * 128],
                            k_fm[hb:hb + 64, ch * T + k0: ch * T + k0 + kw],
                            start=True, stop=True),
                          reads=[r_qh[h][qt // 4], r_kh[h]], writes=[rs])
                        halves.append((ps, rs, kw))
                    w_tot = 1024 if len(halves) == 2 else 512
                    onS = (pq == 1) or (npair == 1 and qt % 2 == 1)
                    pq += 1
                    if onS:
                        E("S", lambda e, w_tot=w_tot, qt=qt, hp=hp, pair=pair: e.activation(
                            dumpW[:, 0:w_tot], pp[pair][:, 0:w_tot], AF.Exp,
                            scale=1.0 / 64.0,
                            accum_out=sumch[:, hp * 64 + qt * 4 + 3: hp * 64 + qt * 4 + 4]),
                          reads=[halves[0][1], halves[-1][1]],
                          writes=[r_dumpW[0], r_sumch[hp][qt]])
                    else:
                        E("V", lambda e, w_tot=w_tot, qt=qt, hp=hp, pair=pair: e.reduce_max(
                            sumch[:, hp * 64 + qt * 4 + 2: hp * 64 + qt * 4 + 3],
                            pp[pair][:, 0:w_tot], axis=AX.X),
                          reads=[halves[0][1], halves[-1][1]],
                          writes=[r_sumch[hp][qt]])
                    yield
            drain_pv(-1)
            rall = [r_sumch[hp][qt] for qt in range(ntt)]

            def sc_ap(slot):
                return bass.AP(sumch, hp * 64 + slot, [[128, 128], [4, ntt]])

            cq = cqb
            E("V", lambda e, h=h: e.tensor_max(
                sumq[:, h * ntt:(h + 1) * ntt], sc_ap(0), sc_ap(1)),
              reads=rall, writes=[r_sumq])
            E("V", lambda e, h=h: e.tensor_max(
                sumq[:, h * ntt:(h + 1) * ntt], sumq[:, h * ntt:(h + 1) * ntt], sc_ap(2)),
              reads=rall + [r_sumq], writes=[r_sumq])
            E("S", lambda e, h=h: e.activation(
                cq[:, h * ntt:(h + 1) * ntt], sc_ap(3), AF.Ln),
              reads=rall, writes=[r_cqb[h]])
            E("V", lambda e, h=h: e.scalar_tensor_tensor(
                cq[:, h * ntt:(h + 1) * ntt], cq[:, h * ntt:(h + 1) * ntt], 64.0,
                sumq[:, h * ntt:(h + 1) * ntt], ALU.mult, ALU.max),
              reads=[r_cqb[h], r_sumq], writes=[r_cqb[h]])
            E("V", lambda e, h=h: e.tensor_scalar(
                cq[:, h * ntt:(h + 1) * ntt], cq[:, h * ntt:(h + 1) * ntt],
                -1.0, None, ALU.mult),
              reads=[r_cqb[h]], writes=[r_cqb[h]])
            yield

        def gen_passB(h):
            ch = h // 2
            hb = (h % 2) * 64
            kslot = (h % 2) * nkt * 128
            for qc in range(nqc):
                drip(3)
                nk = 4 * qc + 4
                for j in range(4):
                    col = h * ntt + qc * 4 + j
                    E("T", lambda e, col=col, j=j: e.transpose(
                        pp[2][0:1, 512 + j * 128: 512 + j * 128 + 128], cqb[:, col:col + 1],
                        idn[:]),
                      reads=[r_cqb[h], r_idn], writes=[r_p[5]])
                sl = (h % 2) * 4 + qc
                E("V", lambda e, sl=sl: e.tensor_copy(
                    qaug[64:65, sl * 512:(sl + 1) * 512], pp[2][0:1, 512:1024]),
                  reads=[r_p[5]], writes=[r_qaug[sl]])
                flush_tail1()
                yield
                prs = list(range(nk // 2 - 1, -1, -1))
                for pi, pr in enumerate(prs):
                    drain_pv(2 * pr)
                    pair = pi % 2
                    for hi in range(2):
                        kt = 2 * pr + hi
                        ps, rs = p[2 * pair + hi], r_p[2 * pair + hi]
                        E("T", lambda e, kt=kt, ps=ps, sl=sl, kslot=kslot: e.matmul(
                            ps[:], kaug[:, kslot + kt * 128: kslot + (kt + 1) * 128],
                            qaug[:, sl * 512:(sl + 1) * 512],
                            start=True, stop=True),
                          reads=[r_kaug[h % 2], r_qaug[sl]], writes=[rs])
                    if pi == 1 or (pi == 0 and len(prs) == 1):
                        flush_tail()
                    kt0 = 2 * pr
                    E("S", lambda e, kt0=kt0, pair=pair: e.activation(
                        ptb[:, kt0 * 512:(kt0 + 2) * 512], pp[pair][:, 0:1024], AF.Exp, scale=0.125),
                      reads=[r_p[2 * pair], r_p[2 * pair + 1]],
                      writes=[r_pt[kt0], r_pt[kt0 + 1]])
                    if kt0 + 1 >= 4 * qc:
                        m0 = kt0 - 4 * qc
                        if m0 >= 0:
                            E("G", lambda e, kt0=kt0, m0=m0: e.tensor_mul(
                                ptb[:, kt0 * 512:(kt0 + 2) * 512], ptb[:, kt0 * 512:(kt0 + 2) * 512],
                                maskb[:, m0 * 512:(m0 + 2) * 512]),
                              reads=[r_pt[kt0], r_pt[kt0 + 1], r_maskb],
                              writes=[r_pt[kt0], r_pt[kt0 + 1]])
                        else:
                            E("G", lambda e, kt0=kt0: e.tensor_mul(
                                ptb[:, (kt0 + 1) * 512:(kt0 + 2) * 512],
                                ptb[:, (kt0 + 1) * 512:(kt0 + 2) * 512], maskb[:, 0:512]),
                              reads=[r_pt[kt0 + 1], r_maskb], writes=[r_pt[kt0 + 1]])
                    yield
                drain_pv(-1)
                po, ro = p[4], r_p[4]
                for ki, kt in enumerate(range(nk - 1, -1, -1)):
                    def fpv(e=None, h=h, kt=kt, po=po, ki=ki, nk=nk, ro=ro):
                        E("T", lambda e, h=h, kt=kt, po=po, ki=ki, nk=nk: e.matmul(
                            po[0:65, 0:512],
                            v_ag[:, kt * (NH * 65) + h * 65: kt * (NH * 65) + (h + 1) * 65],
                            ptb[:, kt * 512:(kt + 1) * 512], start=(ki == 0), stop=(ki == nk - 1)),
                          reads=[r_vag, r_pt[kt]], writes=[ro])
                    pend_pv.append((kt, fpv, ki == 0))
                rp = qc % 2

                def mk_tail1(rp=rp, po=po, ro=ro):
                    def t0():
                        E("V", lambda e: e.reciprocal(
                            recips[0:1, rp * 512:(rp + 1) * 512], po[64:65, 0:512]),
                          reads=[ro], writes=[r_recips[rp]])
                        E("D", lambda e: e.dma_start(
                            d_rbc[rp:rp + 1, :], recips[0:1, rp * 512:(rp + 1) * 512]),
                          reads=[r_recips[rp]], writes=[r_rbc_d[rp]])
                    return [t0]

                def mk_tail2(h=h, qc=qc, rp=rp, ch=ch, hb=hb, po=po, ro=ro):
                    def t1():
                        E("D", lambda e: e.dma_start(
                            rbcast[:, rp * 512:(rp + 1) * 512],
                            bass.AP(d_rbc, rp * 512, [[0, 64], [1, 512]])),
                          reads=[r_rbc_d[rp]], writes=[r_rbcast[rp]])

                    def t2():
                        E("S", lambda e: e.copy(
                            otmp[:, rp * 512:(rp + 1) * 512], po[0:64, 0:512]),
                          reads=[ro], writes=[r_otmp[rp]])

                    def t3():
                        E("V", lambda e: e.scalar_tensor_tensor(
                            q_fm[hb:hb + 64, ch * T + qc * 512: ch * T + (qc + 1) * 512],
                            otmp[:, rp * 512:(rp + 1) * 512], 1.0,
                            rbcast[:, rp * 512:(rp + 1) * 512],
                            ALU.mult, ALU.mult),
                          reads=[r_otmp[rp], r_rbcast[rp]], writes=[r_qh[h][qc]])
                    return [t1, t2, t3]

                def fin(mk1=mk_tail1, mk2=mk_tail2):
                    tail1[0] = mk1()
                    tail2[0] = mk2()
                pend_pv.append((-1, fin, True))
                if DRAIN_NOW:
                    drain_pv(-1)
                yield

        for h in range(NH):
            for _ in gen_passA(h):
                pass
            for _ in gen_passB(h):
                pass
        drain_pv(-1)
        flush_tail()
        drip(len(wprep))

        for tc in range(nqc):
            for oc in range(NIC):
                pw, rw = p[oc % 2], r_p[oc % 2]
                for dc in range(NIC):
                    E("T", lambda e, oc=oc, tc=tc, dc=dc, pw=pw: e.matmul(
                        pw[:], woutT[:, dc * D + oc * 128: dc * D + (oc + 1) * 128],
                        q_fm[:, dc * T + tc * 512: dc * T + (tc + 1) * 512],
                        start=(dc == 0), stop=(dc == NIC - 1)),
                      reads=[r_woutT, r_qh[2 * dc][tc], r_qh[2 * dc + 1][tc]], writes=[rw])
                sb = oc % 2
                E("V", lambda e, pw=pw, sb=sb: e.tensor_copy(afst[:, sb * 512:(sb + 1) * 512], pw[:]),
                  reads=[rw], writes=[r_afst[sb]])
                pt_, rt_ = p[2 + sb], r_p[2 + sb]
                for j in range(4):
                    E("T", lambda e, sb=sb, j=j, pt_=pt_: e.transpose(
                        pt_[:, j * 128:(j + 1) * 128],
                        afst[:, sb * 512 + j * 128: sb * 512 + (j + 1) * 128], idn[:]),
                      reads=[r_afst[sb], r_idn], writes=[rt_])
                for j in range(4):
                    E("S", lambda e, oc=oc, j=j, pt_=pt_: e.copy(
                        astage[:, j * D + oc * 128: j * D + (oc + 1) * 128],
                        pt_[:, j * 128:(j + 1) * 128]),
                      reads=[rt_], writes=[r_astage])
            for j in range(4):
                tt = tc * 4 + j
                E("D", lambda e, tt=tt, j=j: e.dma_start(
                    d_attn[tt * 128:(tt + 1) * 128, :], astage[:, j * D:(j + 1) * D]),
                  reads=[r_astage], writes=[r_attnd[tt]])

        ctx.barrier()

        b0q = []

        def q(fn):
            b0q.append(fn)

        def cast_load(dram, rows, cols, dst, rdst, dst_cols, defer=False):
            for icx in range(rows // 128):
                sb = icx % 2

                def f(icx=icx, sb=sb, dram=dram, cols=cols, dst=dst, rdst=rdst,
                      dst_cols=dst_cols):
                    E("D", lambda e, icx=icx, sb=sb, dram=dram, cols=cols: e.dma_start(
                        wstage3[:, sb * D: sb * D + cols], dram[icx * 128:(icx + 1) * 128, :]),
                      writes=[r_wstage3[sb]])
                    E("V", lambda e, icx=icx, sb=sb, dst=dst, cols=cols, dst_cols=dst_cols: e.tensor_copy(
                        dst[:, icx * dst_cols: icx * dst_cols + cols], wstage3[:, sb * D: sb * D + cols]),
                      reads=[r_wstage3[sb]], writes=[rdst])
                if defer:
                    q(f)
                else:
                    f()

        def state0(tt):
            xb = tt % 2
            E("D", lambda e, tt=tt, xb=xb: e.dma_start(
                xload[:, xb * D:(xb + 1) * D], d_x[tt * 128:(tt + 1) * 128, :]),
              writes=[r_xload[xb]])
            E("V", lambda e, tt=tt, xb=xb: e.tensor_scalar(
                state[:, tt * D:(tt + 1) * D], xload[:, xb * D:(xb + 1) * D],
                mcol[:, tt:tt + 1], rstd[:, tt:tt + 1], ALU.subtract, ALU.mult),
              reads=[r_xload[xb], r_mcol[xb], r_rstd[xb]], writes=[r_state[tt]])

        for tt in range(min(4, ntt)):
            state0(tt)
        for tt in range(4, ntt):
            q(lambda tt=tt: state0(tt))
        cast_load(d_cn1, D, D, cn1b, r_cn1b, D)
        cast_load(d_cn2, D, 512, cn2b, r_cn2b, 512, defer=True)
        cast_load(d_gc1, D, 512, gc1b, r_gc1b, 512, defer=True)
        cast_load(d_cn3, 512, 256, cn3b, r_cn3b, 256, defer=True)
        cast_load(d_gc2, 512, 256, gc2b, r_gc2b, 256, defer=True)
        cast_load(d_cn4, 128, 2, cn4b, r_cn4b, 2, defer=True)
        cast_load(d_gc3, 128, 2, gc3b, r_gc3b, 2, defer=True)
        for oc in range(4):
            def f1(oc=oc):
                for ic in range(NIC):
                    E("T", lambda e, oc=oc, ic=ic: e.matmul(
                        p[5][:, oc:oc + 1], gc1b[:, ic * 512 + oc * 128: ic * 512 + (oc + 1) * 128],
                        gmeanb[:, ic:ic + 1], start=(ic == 0), stop=(ic == NIC - 1)),
                      reads=[r_gc1b, r_gmeanb], writes=[r_p[5]])
                E("S", lambda e, oc=oc: e.activation(
                    gb1[:, oc:oc + 1], p[5][:, oc:oc + 1], AF.Gelu, bias=biasb[:, 14 + oc:15 + oc]),
                  reads=[r_p[5], r_biasb], writes=[r_gb1])
            q(f1)
        for oc in range(2):
            def f2(oc=oc):
                for ic in range(4):
                    E("T", lambda e, oc=oc, ic=ic: e.matmul(
                        p[5][:, 8 + oc:9 + oc], gc2b[:, ic * 256 + oc * 128: ic * 256 + (oc + 1) * 128],
                        gb1[:, ic:ic + 1], start=(ic == 0), stop=(ic == 3)),
                      reads=[r_gc2b, r_gb1], writes=[r_p[5]])
                E("S", lambda e, oc=oc: e.activation(
                    gb2[:, oc:oc + 1], p[5][:, 8 + oc:9 + oc], AF.Gelu, bias=biasb[:, 18 + oc:19 + oc]),
                  reads=[r_p[5], r_biasb], writes=[r_gb2])
            q(f2)

        def f3():
            for ic in range(2):
                E("T", lambda e, ic=ic: e.matmul(
                    p[5][0:1, 12:13], gc3b[:, ic:ic + 1], gb2[:, ic:ic + 1],
                    start=(ic == 0), stop=(ic == 1)),
                  reads=[r_gc3b, r_gb2], writes=[r_p[5]])
            E("S", lambda e: e.activation(ab2[0:1, 2:3], p[5][0:1, 12:13], AF.Sigmoid, bias=scal3b[0:1, 1:2]),
              reads=[r_p[5], r_scal3], writes=[r_ab2])
            E("V", lambda e: e.tensor_scalar(ab2[0:1, 3:4], ab2[0:1, 2:3], 1.4, -1.0, ALU.mult, ALU.add),
              reads=[r_ab2], writes=[r_ab2])
            E("V", lambda e: e.tensor_mul(ab2[0:1, 4:5], ab2[0:1, 3:4], scal3b[0:1, 2:3]),
              reads=[r_ab2, r_scal3], writes=[r_ab2])
            E("V", lambda e: e.tensor_scalar(ab2[0:1, 0:1], ab2[0:1, 4:5], 1.0, 0.1, ALU.mult, ALU.add),
              reads=[r_ab2], writes=[r_ab2])
            E("V", lambda e: e.tensor_scalar(ab2[0:1, 1:2], scal3b[0:1, 2:3], 0.6, None, ALU.mult),
              reads=[r_scal3], writes=[r_ab2])
            E("T", lambda e: e.matmul(p[5][0:128, 16:18], ones1[:], ab2[0:1, 0:2], start=True, stop=True),
              reads=[r_ones1, r_ab2], writes=[r_p[5]])
            E("V", lambda e: e.tensor_copy(abb[:], p[5][0:128, 16:18]), reads=[r_p[5]], writes=[r_abb])
        q(f3)

        def drain_b0(n):
            for _ in range(n):
                if b0q:
                    b0q.pop(0)()

        def gen_ffn_A(cyc, tc):
            sp = tc % 2
            for j in range(4):
                tt = tc * 4 + j
                jb = j % 2
                E("V", lambda e, tt=tt, jb=jb: e.tensor_copy(
                    sbtm[:, jb * D:(jb + 1) * D], state[:, tt * D:(tt + 1) * D]),
                  reads=[r_state[tt]], writes=[r_sbtm])
                for half in range(2):
                    pb, rb = p[6 + half], r_p[6 + half]
                    for jj in range(4):
                        ic2 = half * 4 + jj
                        E("T", lambda e, jb=jb, ic2=ic2, jj=jj, pb=pb: e.transpose(
                            pb[:, jj * 128:(jj + 1) * 128],
                            sbtm[:, jb * D + ic2 * 128: jb * D + (ic2 + 1) * 128], idnb[:]),
                          reads=[r_sbtm, r_idnb], writes=[rb])
                    E("V", lambda e, half=half, j=j, pb=pb, sp=sp: e.tensor_copy(
                        bass.AP(sfm, sp * NIC * 512 + half * 4 * 512 + j * 128,
                                [[2 * NIC * 512, 128], [512, 4], [1, 128]]),
                        bass.AP(pb, 0, [[1024, 128], [128, 4], [1, 128]])),
                      reads=[rb], writes=[r_sfm[sp]])
                    yield

        def gen_ffn_B(cyc, tc):
            sp = tc % 2
            if True:
                for oc in range(8):
                    ph, rhh = p[oc % 4], r_p[oc % 4]
                    for ic in range(NIC):
                        E("T", lambda e, oc=oc, ic=ic, ph=ph, sp=sp: e.matmul(
                            ph[:], cn1b[:, ic * D + oc * 128: ic * D + (oc + 1) * 128],
                            sfm[:, sp * NIC * 512 + ic * 512: sp * NIC * 512 + (ic + 1) * 512],
                            start=(ic == 0), stop=(ic == NIC - 1)),
                          reads=[r_cn1b, r_sfm[sp]], writes=[rhh])
                    E("S", lambda e, oc=oc, ph=ph: e.activation(
                        h1b[:, oc * 512:(oc + 1) * 512], ph[:], AF.Gelu, bias=biasb[:, oc:oc + 1]),
                      reads=[rhh, r_biasb], writes=[r_h1])
                    yield
                for oc in range(4):
                    ph, rhh = p[oc % 4], r_p[oc % 4]
                    for ic in range(NIC):
                        E("T", lambda e, oc=oc, ic=ic, ph=ph: e.matmul(
                            ph[:], cn2b[:, ic * 512 + oc * 128: ic * 512 + (oc + 1) * 128],
                            h1b[:, ic * 512:(ic + 1) * 512], start=(ic == 0), stop=(ic == NIC - 1)),
                          reads=[r_cn2b, r_h1], writes=[rhh])
                    E("S", lambda e, oc=oc, ph=ph: e.activation(
                        h2b[:, oc * 512:(oc + 1) * 512], ph[:], AF.Gelu, bias=biasb[:, 8 + oc:9 + oc]),
                      reads=[rhh, r_biasb], writes=[r_h2])
                    yield
                for oc in range(2):
                    ph, rhh = p[oc % 4], r_p[oc % 4]
                    for ic in range(4):
                        E("T", lambda e, oc=oc, ic=ic, ph=ph: e.matmul(
                            ph[:], cn3b[:, ic * 256 + oc * 128: ic * 256 + (oc + 1) * 128],
                            h2b[:, ic * 512:(ic + 1) * 512], start=(ic == 0), stop=(ic == 3)),
                          reads=[r_cn3b, r_h2], writes=[rhh])
                    E("S", lambda e, oc=oc, ph=ph: e.activation(
                        h3b[:, oc * 512:(oc + 1) * 512], ph[:], AF.Gelu, bias=biasb[:, 12 + oc:13 + oc]),
                      reads=[rhh, r_biasb], writes=[r_h3])
                    yield
                for j in range(4):
                    for ic in range(2):
                        E("T", lambda e, j=j, ic=ic: e.matmul(
                            p[4][:, j:j + 1], h3b[:, ic * 512 + j * 128: ic * 512 + (j + 1) * 128],
                            cn4b[:, ic:ic + 1], start=(ic == 0), stop=(ic == 1)),
                          reads=[r_h3, r_cn4b], writes=[r_p[4]])
                drain_b0(999)
                E("V", lambda e, tc=tc: e.tensor_copy(tfb[:, tc * 4: tc * 4 + 4], p[4][:, 0:4]),
                  reads=[r_p[4]], writes=[r_tfb])
                E("S", lambda e, tc=tc: e.activation(
                    tfsg[:, tc * 4: tc * 4 + 4], tfb[:, tc * 4: tc * 4 + 4], AF.Sigmoid,
                    bias=scalB[:, 0:1]),
                  reads=[r_tfb, r_scalB], writes=[r_tfsg])
                E("V", lambda e, tc=tc: e.tensor_scalar(
                    coup[:, tc * 4: tc * 4 + 4], tfsg[:, tc * 4: tc * 4 + 4],
                    abb[:, 1:2], abb[:, 0:1], ALU.mult, ALU.add),
                  reads=[r_tfsg, r_abb], writes=[r_coup])
                E("V", lambda e, tc=tc: e.tensor_scalar(
                    ncoup[:, tc * 4: tc * 4 + 4], coup[:, tc * 4: tc * 4 + 4], -1.0, None, ALU.mult),
                  reads=[r_coup], writes=[r_ncoup])
                yield
                def tw_op(step, b, tt):
                    if step == 0:
                        return lambda: E("V", lambda e, b=b, tt=tt: e.tensor_scalar(
                            tmpq[:, (tt % 2) * 256:(tt % 2) * 256 + 256],
                            state[:, b + 512:b + 768], coup[:, tt:tt + 1], None, ALU.mult),
                          reads=[r_state[tt], r_coup], writes=[r_tmpq[tt % 2]])
                    elif step == 1:
                        return lambda: E("V", lambda e, b=b, tt=tt: e.scalar_tensor_tensor(
                            state[:, b + 512:b + 768], state[:, b + 0:b + 256], ncoup[:, tt:tt + 1],
                            state[:, b + 512:b + 768], ALU.mult, ALU.add),
                          reads=[r_state[tt], r_ncoup], writes=[r_state[tt]])
                    elif step == 2:
                        return lambda: E("V", lambda e, b=b, tt=tt: e.tensor_add(
                            state[:, b + 0:b + 256], state[:, b + 0:b + 256],
                            tmpq[:, (tt % 2) * 256:(tt % 2) * 256 + 256]),
                          reads=[r_tmpq[tt % 2], r_state[tt]], writes=[r_state[tt]])
                    elif step == 3:
                        return lambda: E("V", lambda e, b=b, tt=tt: e.tensor_scalar(
                            tmpq[:, (tt % 2) * 256:(tt % 2) * 256 + 256],
                            state[:, b + 768:b + 1024], coup[:, tt:tt + 1], None, ALU.mult),
                          reads=[r_state[tt], r_coup], writes=[r_tmpq[tt % 2]])
                    elif step == 4:
                        return lambda: E("V", lambda e, b=b, tt=tt: e.scalar_tensor_tensor(
                            state[:, b + 768:b + 1024], state[:, b + 256:b + 512], coup[:, tt:tt + 1],
                            state[:, b + 768:b + 1024], ALU.mult, ALU.add),
                          reads=[r_state[tt], r_coup], writes=[r_state[tt]])
                    else:
                        return lambda: E("V", lambda e, b=b, tt=tt: e.tensor_sub(
                            state[:, b + 256:b + 512], state[:, b + 256:b + 512],
                            tmpq[:, (tt % 2) * 256:(tt % 2) * 256 + 256]),
                          reads=[r_tmpq[tt % 2], r_state[tt]], writes=[r_state[tt]])
                for tp in (tc * 4, tc * 4 + 2):
                    for step in range(6):
                        for tt in (tp, tp + 1):
                            pend_tw.append(tw_op(step, tt * D, tt))

        units = [(c, t) for c in range(3) for t in range(nqc)]
        look = nqc > 1
        pend_tw = []

        def drain_tw(n):
            for _ in range(n):
                if pend_tw:
                    pend_tw.pop(0)()

        for _ in gen_ffn_A(*units[0]):
            drain_b0(2)
        for ui, (c, t) in enumerate(units):
            gb = gen_ffn_B(c, t)
            ga = gen_ffn_A(*units[ui + 1]) if look and ui + 1 < len(units) else None
            if nqc == 2:
                drain_tw(99)
            for _ in gb:
                if ga is not None:
                    next(ga, None)
                drain_tw(2)
                drain_b0(3)
            if ga is not None:
                for _ in ga:
                    pass
            if not look and ui + 1 < len(units):
                drain_tw(99)
                for _ in gen_ffn_A(*units[ui + 1]):
                    pass
        drain_tw(99)

        for tt in range(ntt):
            xb = tt % 2
            E("D", lambda e, tt=tt, xb=xb: e.dma_start(
                xload[:, xb * D:(xb + 1) * D], d_x[tt * 128:(tt + 1) * 128, :]),
              writes=[r_xload[xb]])
            E("D", lambda e, tt=tt, xb=xb: e.dma_start(
                sbtm[:, 0:D].bitcast(F32) if False else wstage3[:, xb * D:(xb + 1) * D],
                d_attn[tt * 128:(tt + 1) * 128, :]),
              reads=[r_attnd[tt]], writes=[r_wstage3[xb]])
            E("V", lambda e, tt=tt, xb=xb: e.tensor_add(
                state[:, tt * D:(tt + 1) * D], state[:, tt * D:(tt + 1) * D],
                xload[:, xb * D:(xb + 1) * D]),
              reads=[r_xload[xb], r_state[tt]], writes=[r_state[tt]])
            E("V", lambda e, tt=tt, xb=xb: e.tensor_add(
                state[:, tt * D:(tt + 1) * D], state[:, tt * D:(tt + 1) * D],
                wstage3[:, xb * D:(xb + 1) * D]),
              reads=[r_wstage3[xb], r_state[tt]], writes=[r_state[tt]])
            E("D", lambda e, tt=tt: e.dma_start(
                d_y[tt * 128:(tt + 1) * 128, :], state[:, tt * D:(tt + 1) * D]),
              reads=[r_state[tt]], writes=[r_y[tt]])

    from contextlib import ExitStack
    with ExitStack() as es:
        for ename in ctx.sem_names():
            ctx.sems[ename] = es.enter_context(nc.semaphore(f"sem_{ename}"))
        block = es.enter_context(nc.Block())

        ctx.begin("dry")
        script()
        final_snap = ctx._snap()

        def run_for(eng):
            def body(proxy):
                ctx.begin("emit", eng, proxy)
                script()
                if eng == "D":
                    for i in range(ctx.NDMA):
                        if final_snap[f"D{i}"] > 0:
                            proxy.wait_ge(ctx.sems[f"D{i}"], final_snap[f"D{i}"])
            return body

        block.tensor(run_for("T"))
        block.vector(run_for("V"))
        block.scalar(run_for("S"))
        block.sync(run_for("D"))
        block.gpsimd(run_for("G"))

    for n in list(cms)[::-1]:
        free(n)
    for cm in pcms[::-1]:
        cm.__exit__(None, None, None)
    return nc


_PROG_CACHE = {}


def _get_prog(T):
    if T not in _PROG_CACHE:
        _PROG_CACHE[T] = build_program(T)
    return _PROG_CACHE[T]


def make_in_maps(T, x_full, W):
    cos_s, sin_s, masks_bf, maskS, idn, idnb = host_constants(T)
    biasp = pack_biases(W["cn_b1"], W["cn_b2"], W["cn_b3"], W["gc_b1"], W["gc_b2"])
    scal3 = np.array([[np.asarray(W["cn_b4"]).reshape(-1)[0],
                       np.asarray(W["gc_b3"]).reshape(-1)[0],
                       np.asarray(W["ar"]).reshape(-1)[0]]], dtype=np.float32)
    cn4p = np.ascontiguousarray(np.asarray(W["cn_w4"], dtype=np.float32).reshape(2, 128).T)
    gc3p = np.ascontiguousarray(np.asarray(W["gc_w3"], dtype=np.float32).reshape(2, 128).T)
    shared = {
        "wqkv": np.ascontiguousarray(np.asarray(W["wqkv"], dtype=np.float32)),
        "wout": np.ascontiguousarray(np.asarray(W["wout"], dtype=np.float32)),
        "cn1": np.ascontiguousarray(np.asarray(W["cn_w1"], dtype=np.float32)),
        "cn2": np.ascontiguousarray(np.asarray(W["cn_w2"], dtype=np.float32)),
        "cn3": np.ascontiguousarray(np.asarray(W["cn_w3"], dtype=np.float32)),
        "cn4p": cn4p,
        "gc1": np.ascontiguousarray(np.asarray(W["gc_w1"], dtype=np.float32)),
        "gc2": np.ascontiguousarray(np.asarray(W["gc_w2"], dtype=np.float32)),
        "gc3p": gc3p,
        "biasp": biasp,
        "scal3": scal3,
        "cosb": cos_s,
        "sinb": sin_s,
        "maskp": masks_bf,
        "masksq": maskS,
        "idn": idn,
        "idnb": idnb,
    }
    maps = []
    for b in range(x_full.shape[0]):
        m = dict(shared)
        m["x"] = np.ascontiguousarray(np.asarray(x_full[b], dtype=np.float32))
        maps.append(m)
    return maps


def kernel(L0_attn_x, L0_attn_W_qkv, L0_attn_W_out,
           L0_ffn_cn_w1, L0_ffn_cn_b1, L0_ffn_cn_w2, L0_ffn_cn_b2,
           L0_ffn_cn_w3, L0_ffn_cn_b3, L0_ffn_cn_w4, L0_ffn_cn_b4,
           L0_ffn_gc_w1, L0_ffn_gc_b1, L0_ffn_gc_w2, L0_ffn_gc_b2,
           L0_ffn_gc_w3, L0_ffn_gc_b3, L0_ffn_adaptive_range):
    x = np.asarray(L0_attn_x, dtype=np.float32)
    B, T, _ = x.shape
    W = {
        "wqkv": L0_attn_W_qkv, "wout": L0_attn_W_out,
        "cn_w1": L0_ffn_cn_w1, "cn_b1": L0_ffn_cn_b1,
        "cn_w2": L0_ffn_cn_w2, "cn_b2": L0_ffn_cn_b2,
        "cn_w3": L0_ffn_cn_w3, "cn_b3": L0_ffn_cn_b3,
        "cn_w4": L0_ffn_cn_w4, "cn_b4": L0_ffn_cn_b4,
        "gc_w1": L0_ffn_gc_w1, "gc_b1": L0_ffn_gc_b1,
        "gc_w2": L0_ffn_gc_w2, "gc_b2": L0_ffn_gc_b2,
        "gc_w3": L0_ffn_gc_w3, "gc_b3": L0_ffn_gc_b3,
        "ar": L0_ffn_adaptive_range,
    }
    nc = _get_prog(T)
    in_maps = make_in_maps(T, x, W)
    res = run_bass_kernel_spmd(nc, in_maps, core_ids=list(range(B)))
    out = np.stack([np.asarray(res.results[b]["y"], dtype=np.float32) for b in range(B)], axis=0)
    return out


# version 41
# speedup vs baseline: 1.0060x; 1.0060x over previous
import numpy as np
import ml_dtypes
import concourse.bass as bass
import concourse.mybir as mybir
from concourse.bass_utils import run_bass_kernel_spmd

F32 = mybir.dt.float32
BF16 = mybir.dt.bfloat16
AF = mybir.ActivationFunctionType
ALU = mybir.AluOpType
AX = mybir.AxisListType

D = 1024
NH = 16
HD = 64
NIC = D // 128

RELAX_ENGINES = {"T", "S"}
DRAIN_NOW = False


_ALL_RES = []


class Res:
    __slots__ = ("last_w", "readers")

    def __init__(self):
        self.last_w = None
        self.readers = []
        _ALL_RES.append(self)


class Ctx:
    ENGINES = ("T", "V", "S", "D", "G")
    NDMA = 8

    def __init__(self):
        self.tokens = []
        self.counts = {e: 0 for e in ("T", "V", "S", "G")}
        self.dma_i = 0
        self.sems = {}
        self.mode = "dry"
        self.emit_eng = None
        self.proxy = None
        self.idx = 0
        self.watermark = {}

    def sem_names(self):
        return ["T", "V", "S", "G"] + [f"D{i}" for i in range(self.NDMA)]

    def begin(self, mode, eng=None, proxy=None):
        self.mode = mode
        self.emit_eng = eng
        self.proxy = proxy
        self.idx = 0
        self.dma_i = 0
        self.counts = {e: 0 for e in ("T", "V", "S", "G")}
        self.watermark = {s: 0 for s in self.sem_names()}
        for r in _ALL_RES:
            r.last_w = None
            r.readers = []

    def _wait(self, sem, val):
        if val <= 0 or self.watermark[sem] >= val:
            return
        self.watermark[sem] = val
        self.proxy.wait_ge(self.sems[sem], val)

    def emit(self, eng, fn, reads=(), writes=()):
        i = self.idx
        self.idx += 1
        if eng == "D":
            di = self.dma_i
            self.dma_i += 1
            pool = f"D{di % self.NDMA}"
            tok = (pool, 16 * (di // self.NDMA + 1))
        else:
            if self.mode == "dry":
                self.counts[eng] += 1
            tok = (eng, (self.counts[eng] if self.mode == "dry" else None))
        if self.mode == "dry":
            if eng == "D":
                self.tokens.append(tok)
            else:
                self.tokens.append(tok)
        else:
            tok = self.tokens[i]
            if eng == self.emit_eng:
                deps = []
                for r in reads:
                    if r.last_w is not None:
                        deps.append(r.last_w)
                for w in writes:
                    if w.last_w is not None:
                        deps.append(w.last_w)
                    deps.extend(w.readers)
                if eng == "D":
                    deps.append((tok[0], tok[1] - 16))
                for de, dv in deps:
                    if de == eng and dv >= tok[1]:
                        continue
                    if de == eng and eng in RELAX_ENGINES:
                        continue
                    self._wait(de, dv)
                instr = fn(self.proxy)
                instr.then_inc(self.sems[tok[0]], 16 if eng == "D" else 1)
        for r in reads:
            r.readers.append(tok)
        for w in writes:
            w.last_w = tok
            w.readers = []
        return tok

    def _snap(self):
        s = {e: self.counts[e] for e in ("T", "V", "S", "G")}
        for i in range(self.NDMA):
            n = self.dma_i
            s[f"D{i}"] = 16 * ((n - i + self.NDMA - 1) // self.NDMA)
        return s

    def barrier(self):
        i = self.idx
        self.idx += 1
        if self.mode == "dry":
            self.tokens.append(("BARRIER", self._snap()))
            return
        snap = self.tokens[i][1]
        for sname, v in snap.items():
            if sname != self.emit_eng:
                self._wait(sname, v)


def host_constants(T):
    ntt = T // 128
    freqs = (1.0 / (np.float32(10000.0) ** (np.arange(0, HD, 2, dtype=np.float32) / np.float32(HD)))).astype(np.float32)
    t = np.arange(T, dtype=np.float32)
    ang = np.outer(t, freqs).astype(np.float32)
    cos = np.cos(ang).astype(np.float32)
    sin = np.sin(ang).astype(np.float32)
    cos_s = np.ascontiguousarray(cos.reshape(ntt, 128, 32).transpose(1, 0, 2).reshape(128, ntt * 32))
    sin_s = np.ascontiguousarray(sin.reshape(ntt, 128, 32).transpose(1, 0, 2).reshape(128, ntt * 32))
    masks = np.zeros((128, 4 * 512), dtype=np.float32)
    kk = np.arange(128)[:, None]
    qq = np.arange(512)[None, :]
    for m in range(4):
        valid = (qq >= 128 * m + kk) | (qq >= 128 * (m + 1))
        dead = qq < 128 * m
        masks[:, m * 512:(m + 1) * 512] = np.where(dead, 0.0, np.where(valid, 1.0, 0.0))
    masks_bf = masks.astype(ml_dtypes.bfloat16)
    maskS = np.where(np.arange(128)[None, :] > np.arange(128)[:, None], -30000.0, 0.0).astype(np.float32)
    idn = np.eye(128, dtype=np.float32)
    idnb = idn.astype(ml_dtypes.bfloat16)
    return cos_s, sin_s, masks_bf, maskS, idn, idnb


def pack_biases(cn_b1, cn_b2, cn_b3, gc_b1, gc_b2):
    out = np.zeros((128, 20), dtype=np.float32)
    out[:, 0:8] = np.asarray(cn_b1).reshape(8, 128).T
    out[:, 8:12] = np.asarray(cn_b2).reshape(4, 128).T
    out[:, 12:14] = np.asarray(cn_b3).reshape(2, 128).T
    out[:, 14:18] = np.asarray(gc_b1).reshape(4, 128).T
    out[:, 18:20] = np.asarray(gc_b2).reshape(2, 128).T
    return out


def build_program(T):
    _ALL_RES.clear()
    ntt = T // 128
    nqc = T // 512
    nkt = T // 128

    nc = bass.Bass("TRN2", target_bir_lowering=False,
                   detect_race_conditions=(not RELAX_ENGINES))

    d_x = nc.dram_tensor("x", [T, D], F32, kind="ExternalInput")
    d_wqkv = nc.dram_tensor("wqkv", [3 * D, D], F32, kind="ExternalInput")
    d_wout = nc.dram_tensor("wout", [D, D], F32, kind="ExternalInput")
    d_cn1 = nc.dram_tensor("cn1", [D, D], F32, kind="ExternalInput")
    d_cn2 = nc.dram_tensor("cn2", [D, 512], F32, kind="ExternalInput")
    d_cn3 = nc.dram_tensor("cn3", [512, 256], F32, kind="ExternalInput")
    d_cn4 = nc.dram_tensor("cn4p", [128, 2], F32, kind="ExternalInput")
    d_gc1 = nc.dram_tensor("gc1", [D, 512], F32, kind="ExternalInput")
    d_gc2 = nc.dram_tensor("gc2", [512, 256], F32, kind="ExternalInput")
    d_gc3 = nc.dram_tensor("gc3p", [128, 2], F32, kind="ExternalInput")
    d_biasp = nc.dram_tensor("biasp", [128, 20], F32, kind="ExternalInput")
    d_scal3 = nc.dram_tensor("scal3", [1, 3], F32, kind="ExternalInput")
    d_cos = nc.dram_tensor("cosb", [128, ntt * 32], F32, kind="ExternalInput")
    d_sin = nc.dram_tensor("sinb", [128, ntt * 32], F32, kind="ExternalInput")
    d_mask = nc.dram_tensor("maskp", [128, 4 * 512], BF16, kind="ExternalInput")
    d_masksq = nc.dram_tensor("masksq", [128, 128], F32, kind="ExternalInput")
    d_idn = nc.dram_tensor("idn", [128, 128], F32, kind="ExternalInput")
    d_idnb = nc.dram_tensor("idnb", [128, 128], BF16, kind="ExternalInput")
    d_attn = nc.dram_tensor("attn_scr", [T, D], F32, kind="Internal")
    d_rbc = nc.dram_tensor("rbc_scr", [2, 512], F32, kind="Internal")
    d_y = nc.dram_tensor("y", [T, D], F32, kind="ExternalOutput")

    cms = {}

    def alloc(name, shape, dtype):
        cm = nc.sbuf_tensor(name, shape, dtype)
        h = cm.__enter__()
        cms[name] = cm
        return h

    def free(*names):
        for n in names:
            cms.pop(n).__exit__(None, None, None)

    xload = alloc("xload", [128, 2 * D], F32)
    sums = alloc("sums", [128, ntt], F32)
    ssq = alloc("ssq", [128, ntt], F32)
    mcol = alloc("mcol", [128, ntt], F32)
    rstd = alloc("rstd", [128, ntt], F32)
    vtmp = alloc("vtmp", [128, 8], F32)
    epsb = alloc("epsb", [128, 1], F32)
    biasb = alloc("biasb", [128, 20], F32)
    scal3b = alloc("scal3b", [1, 3], F32)
    scalB = alloc("scalB", [128, 3], F32)
    ones1 = alloc("ones1", [1, 128], F32)
    gmean = alloc("gmean", [128, NIC], F32)
    gmeanb = alloc("gmeanb", [128, NIC], BF16)
    idn = alloc("idn_s", [128, 128], F32)
    idnb = alloc("idnb_s", [128, 128], BF16)
    qkm = alloc("qkm", [128, 64], F32)
    redqk = alloc("redqk", [64, 1], F32)
    crow = alloc("crow", [1, 64], F32)
    negc = alloc("negc", [128, NH], F32)
    afst = alloc("afst", [128, 2 * 512], F32)

    q_fm = alloc("q_fm", [128, NIC * T], BF16)
    k_fm = alloc("k_fm", [128, NIC * T], BF16)
    v_ag = alloc("v_ag", [128, ntt * (NH * 65)], BF16)
    maskb = alloc("maskb", [128, 4 * 512], BF16)
    otmp = alloc("otmp", [64, 2 * 512], BF16)
    recips = alloc("recips", [1, 2 * 512], F32)
    rbcast = alloc("rbcast", [64, 2 * 512], F32)

    scr = alloc("scr", [128, D], F32)
    scr2 = alloc("scr2x", [128, 64], F32)
    cosb = alloc("cosb_s", [128, ntt * 32], F32)
    sinb = alloc("sinb_s", [128, ntt * 32], F32)
    xn_tm = alloc("xn_tm", [128, D], BF16)
    xn_fm = alloc("xn_fm", [128, NIC * T], BF16)
    wstage = alloc("wstage", [128, D], F32)
    wcast = alloc("wcast", [128, D], BF16)
    wT = alloc("wT", [128, 2 * NIC * 512], BF16)
    ropetm = alloc("ropetm", [128, 2 * 512], BF16)
    qss = alloc("qss", [128, NH * ntt], F32)
    kss = alloc("kss", [128, NH * ntt], F32)

    free("kss", "qss", "ropetm", "wT", "wcast", "wstage", "xn_fm", "xn_tm",
         "sinb_s", "cosb_s", "scr2x", "scr")
    ptb = alloc("ptb", [128, nkt * 512], BF16)
    kaug = alloc("kaug", [65, 2 * nkt * 128], BF16)
    qaug = alloc("qaug", [65, 8 * 512], BF16)
    dumpW = alloc("dumpW", [128, 2 * 512], BF16)
    sumch = alloc("sumch", [128, 128], F32)
    sumq = alloc("sumq", [128, NH * ntt], F32)
    cqb = alloc("cqb", [128, NH * ntt], F32)
    maskSb = alloc("maskSb", [128, 128], F32)
    negc8 = alloc("negc8", [128, NH], F32)
    negcd8 = alloc("negcd8", [128, NH], F32)
    woutT = alloc("woutT", [128, NIC * D], BF16)
    wstage2 = alloc("wstage2", [128, 2 * D], F32)
    wcast2 = alloc("wcast2", [128, D], BF16)
    astage = alloc("astage", [128, 4 * D], F32)

    free("astage", "wcast2", "wstage2", "woutT",
         "negcd8", "negc8", "maskSb", "cqb", "sumq", "sumch", "dumpW", "qaug", "kaug", "ptb",
         "rbcast", "recips", "otmp", "maskb", "v_ag", "k_fm", "q_fm")
    state = alloc("state", [128, ntt * D], F32)
    sbtm = alloc("sbtm", [128, 2 * D], BF16)
    sfm = alloc("sfm", [128, 2 * NIC * 512], BF16)
    h1b = alloc("h1b", [128, 8 * 512], BF16)
    h2b = alloc("h2b", [128, 4 * 512], BF16)
    h3b = alloc("h3b", [128, 2 * 512], BF16)
    tfb = alloc("tfb", [128, ntt], F32)
    tfsg = alloc("tfsg", [128, ntt], F32)
    coup = alloc("coup", [128, ntt], F32)
    ncoup = alloc("ncoup", [128, ntt], F32)
    tmpq = alloc("tmpq", [128, 512], F32)
    gb1 = alloc("gb1", [128, 4], BF16)
    gb2 = alloc("gb2", [128, 2], BF16)
    ab2 = alloc("ab2", [1, 8], F32)
    abb = alloc("abb", [128, 2], F32)
    cn1b = alloc("cn1b", [128, 8 * D], BF16)
    cn2b = alloc("cn2b", [128, 8 * 512], BF16)
    cn3b = alloc("cn3b", [128, 4 * 256], BF16)
    cn4b = alloc("cn4b", [128, 2], BF16)
    gc1b = alloc("gc1b", [128, 8 * 512], BF16)
    gc2b = alloc("gc2b", [128, 4 * 256], BF16)
    gc3b = alloc("gc3b", [128, 2], BF16)
    wstage3 = alloc("wstage3", [128, 2 * D], F32)

    pcms = []

    def palloc(name, shape, dtype):
        cm = nc.psum_tensor(name, shape, dtype)
        h = cm.__enter__()
        pcms.append(cm)
        return h

    pp = [palloc(f"pp{i}", [128, 1024], F32) for i in range(3)]
    p6t = palloc("p6", [128, 1024], BF16)
    p7t = palloc("p7", [128, 1024], BF16)

    class PV:
        def __init__(self, t, base):
            self.t = t
            self.base = base

        def __getitem__(self, key):
            if not isinstance(key, tuple):
                return self.t[:, self.base:self.base + 512]
            rows, cols = key
            if cols == slice(None):
                cols = slice(0, 512)
            return self.t[rows, self.base + (cols.start or 0): self.base + cols.stop]

    pv8 = [PV(pp[0], 0), PV(pp[0], 512), PV(pp[1], 0), PV(pp[1], 512),
           PV(pp[2], 0), PV(pp[2], 512)]
    p = pv8 + [p6t, p7t]

    r_xload = [Res(), Res()]
    r_sums = [Res(), Res()]
    r_ssq = [Res(), Res()]
    r_mcol = [Res(), Res()]
    r_rstd = [Res(), Res()]
    r_vtmp = [Res(), Res()]
    r_epsb = Res()
    r_cos, r_sin, r_biasb, r_scal3, r_scalB, r_ones1 = (Res() for _ in range(6))
    r_gmean, r_gmeanb, r_idn, r_idnb, r_scr, r_scr2 = (Res() for _ in range(6))
    r_qkm, r_redqk, r_crow, r_negc = (Res() for _ in range(4))
    r_xn_tm = Res()
    r_xn_fm = Res()
    r_wstage = Res()
    r_wcast = Res()
    r_wT = [Res(), Res()]
    r_ropetm = [Res(), Res()]
    r_qss, r_kss = Res(), Res()
    r_qh = [[Res() for _ in range(T // 512)] for _ in range(NH)]
    r_kh = [Res() for _ in range(NH)]
    r_vag = Res()
    r_pt = [Res() for _ in range(nkt)]
    r_maskb = Res()
    r_recips = [Res(), Res()]
    r_rbc_d = [Res(), Res()]
    r_rbcast = [Res(), Res()]
    r_otmp = [Res(), Res()]
    r_p = [Res() for _ in range(8)]
    r_woutT, r_wcast2 = Res(), Res()
    r_kaug = [Res(), Res()]
    r_qaug = [Res() for _ in range(8)]
    r_dumpW = [Res(), Res()]
    r_sumch = [[Res() for _ in range(16)], [Res() for _ in range(16)]]
    r_sumq, r_maskSb = Res(), Res()
    r_cqb = [Res() for _ in range(NH)]
    r_negc8, r_negcd8 = Res(), Res()
    r_wstage2 = [Res(), Res()]
    r_afst = [Res(), Res()]
    r_astage = Res()
    r_attnd = [Res() for _ in range(ntt)]
    r_state = [Res() for _ in range(ntt)]
    r_sbtm = Res()
    r_sfm = [Res(), Res()]
    r_h1, r_h2, r_h3 = (Res() for _ in range(3))
    r_tfb, r_tfsg, r_coup, r_ncoup = (Res() for _ in range(4))
    r_tmpq = [Res(), Res()]
    r_gb1, r_gb2, r_ab2, r_abb = (Res() for _ in range(4))
    r_cn1b, r_cn2b, r_cn3b, r_cn4b = (Res() for _ in range(4))
    r_gc1b, r_gc2b, r_gc3b = (Res() for _ in range(3))
    r_wstage3 = [Res(), Res()]
    r_y = [Res() for _ in range(ntt)]

    ctx = Ctx()

    def script():
        E = ctx.emit

        E("D", lambda e: e.dma_start(cosb[:], d_cos[:]), writes=[r_cos])
        E("D", lambda e: e.dma_start(sinb[:], d_sin[:]), writes=[r_sin])
        E("D", lambda e: e.dma_start(biasb[:], d_biasp[:]), writes=[r_biasb])
        E("D", lambda e: e.dma_start(scal3b[:], d_scal3[:]), writes=[r_scal3])
        E("D", lambda e: e.dma_start(maskb[:], d_mask[:]), writes=[r_maskb])
        E("D", lambda e: e.dma_start(idn[:], d_idn[:]), writes=[r_idn])
        E("D", lambda e: e.dma_start(idnb[:], d_idnb[:]), writes=[r_idnb])
        E("V", lambda e: e.memset(epsb[:], 1e-5), writes=[r_epsb])
        E("V", lambda e: e.memset(ones1[:], 1.0), writes=[r_ones1])
        for tt in range(ntt):
            E("V", lambda e, tt=tt: e.memset(
                bass.AP(v_ag, tt * (NH * 65) + 64, [[ntt * NH * 65, 128], [65, NH], [1, 1]]), 1.0),
              writes=[r_vag])
        E("T", lambda e: e.matmul(p[5][0:128, 0:3], ones1[:], scal3b[:], start=True, stop=True),
          reads=[r_ones1, r_scal3], writes=[r_p[5]])
        E("V", lambda e: e.tensor_copy(scalB[:], p[5][0:128, 0:3]), reads=[r_p[5]], writes=[r_scalB])

        def ln_tile(tt):
            xb = tt % 2
            E("D", lambda e, tt=tt, xb=xb: e.dma_start(
                xload[:, xb * D:(xb + 1) * D], d_x[tt * 128:(tt + 1) * 128, :]), writes=[r_xload[xb]])
            v0 = xb * 4
            E("V", lambda e, tt=tt, xb=xb: e.reduce_sum(
                sums[:, tt:tt + 1], xload[:, xb * D:(xb + 1) * D], axis=AX.X),
              reads=[r_xload[xb]], writes=[r_sums[xb]])
            E("S", lambda e, tt=tt, xb=xb: e.activation(
                afst[:, 0:D], xload[:, xb * D:(xb + 1) * D], AF.Square, accum_out=ssq[:, tt:tt + 1]),
              reads=[r_xload[xb]], writes=[r_afst[0], r_afst[1], r_ssq[xb]])
            E("V", lambda e, tt=tt, xb=xb: e.tensor_scalar(
                mcol[:, tt:tt + 1], sums[:, tt:tt + 1], 1.0 / D, None, ALU.mult),
              reads=[r_sums[xb]], writes=[r_mcol[xb]])
            E("V", lambda e, tt=tt, v0=v0, xb=xb: e.tensor_mul(
                vtmp[:, v0:v0 + 1], mcol[:, tt:tt + 1], mcol[:, tt:tt + 1]),
              reads=[r_mcol[xb]], writes=[r_vtmp[xb]])
            E("V", lambda e, tt=tt, v0=v0, xb=xb: e.tensor_scalar(
                vtmp[:, v0 + 1:v0 + 2], ssq[:, tt:tt + 1], 1.0 / D, None, ALU.mult),
              reads=[r_ssq[xb]], writes=[r_vtmp[xb]])
            E("V", lambda e, v0=v0, xb=xb: e.tensor_sub(
                vtmp[:, v0 + 2:v0 + 3], vtmp[:, v0 + 1:v0 + 2], vtmp[:, v0:v0 + 1]),
              reads=[r_vtmp[xb]], writes=[r_vtmp[xb]])
            E("S", lambda e, v0=v0, xb=xb: e.activation(
                vtmp[:, v0 + 3:v0 + 4], vtmp[:, v0 + 2:v0 + 3], AF.Sqrt, bias=epsb[:]),
              reads=[r_vtmp[xb], r_epsb], writes=[r_vtmp[xb]])
            E("V", lambda e, tt=tt, v0=v0, xb=xb: e.reciprocal(
                rstd[:, tt:tt + 1], vtmp[:, v0 + 3:v0 + 4]),
              reads=[r_vtmp[xb]], writes=[r_rstd[xb]])
            E("V", lambda e, tt=tt, xb=xb: e.tensor_scalar(
                xn_tm[:], xload[:, xb * D:(xb + 1) * D],
                mcol[:, tt:tt + 1], rstd[:, tt:tt + 1], ALU.subtract, ALU.mult),
              reads=[r_xload[xb], r_mcol[xb], r_rstd[xb]], writes=[r_xn_tm])
            for half in range(2):
                pb, rb = p[6 + half], r_p[6 + half]
                for j in range(4):
                    ic = half * 4 + j
                    E("T", lambda e, ic=ic, j=j, pb=pb: e.transpose(
                        pb[:, j * 128:(j + 1) * 128], xn_tm[:, ic * 128:(ic + 1) * 128], idnb[:]),
                      reads=[r_xn_tm, r_idnb], writes=[rb])
                E("S", lambda e, tt=tt, half=half, pb=pb: e.copy(
                    bass.AP(xn_fm, half * 4 * T + tt * 128, [[NIC * T, 128], [T, 4], [1, 128]]),
                    bass.AP(pb, 0, [[1024, 128], [128, 4], [1, 128]])),
                  reads=[rb], writes=[r_xn_fm])

        for tt in range(min(6, ntt)):
            ln_tile(tt)
        lnq = list(range(6, ntt))

        def emit_gmean():
            for ic in range(NIC):
                E("V", lambda e, ic=ic: e.reduce_sum(
                    gmean[:, ic:ic + 1], xn_fm[:, ic * T:(ic + 1) * T], axis=AX.X),
                  reads=[r_xn_fm], writes=[r_gmean])
            E("V", lambda e: e.tensor_scalar(gmean[:], gmean[:], 1.0 / T, None, ALU.mult),
              reads=[r_gmean], writes=[r_gmean])
            E("V", lambda e: e.tensor_copy(gmeanb[:], gmean[:]), reads=[r_gmean], writes=[r_gmeanb])

        def build_wT(oc):
            ops = []
            wb = oc % 2
            for os_ in range(4):
                ops.append(lambda oc=oc, os_=os_: E("D", lambda e, oc=oc, os_=os_: e.dma_start(
                    wstage[:], d_wqkv[oc * 512 + os_ * 128: oc * 512 + (os_ + 1) * 128, :]),
                  writes=[r_wstage]))
                ops.append(lambda: E("V", lambda e: e.tensor_copy(wcast[:], wstage[:]),
                  reads=[r_wstage], writes=[r_wcast]))
                for half in range(2):
                    pb, rb = p[6 + half], r_p[6 + half]

                    def tr_ev(half=half, os_=os_, pb=pb, rb=rb, wb=wb):
                        for j in range(4):
                            ic = half * 4 + j
                            E("T", lambda e, ic=ic, j=j, pb=pb: e.transpose(
                                pb[:, j * 128:(j + 1) * 128], wcast[:, ic * 128:(ic + 1) * 128], idnb[:]),
                              reads=[r_wcast, r_idnb], writes=[rb])
                        E("S", lambda e, wb=wb, half=half, os_=os_, pb=pb: e.copy(
                            bass.AP(wT, wb * NIC * 512 + half * 4 * 512 + os_ * 128,
                                    [[2 * NIC * 512, 128], [512, 4], [1, 128]]),
                            bass.AP(pb, 0, [[1024, 128], [128, 4], [1, 128]])),
                          reads=[rb], writes=[r_wT[wb]])
                    ops.append(tr_ev)
            return ops

        for op in build_wT(0):
            op()
        for oc in range(6):
            wb = oc % 2
            wbq = build_wT(oc + 1) if oc < 5 else []
            pend = [None]

            def flush_pend():
                if pend[0] is not None:
                    for f in pend[0]:
                        f()
                    pend[0] = None

            for tt in range(ntt):
                pq, rq = p[tt % 4], r_p[tt % 4]
                for ic in range(NIC):
                    E("T", lambda e, tt=tt, ic=ic, wb=wb, pq=pq: e.matmul(
                        pq[:], xn_fm[:, ic * T + tt * 128: ic * T + (tt + 1) * 128],
                        wT[:, wb * NIC * 512 + ic * 512: wb * NIC * 512 + (ic + 1) * 512],
                        start=(ic == 0), stop=(ic == NIC - 1)),
                      reads=[r_xn_fm, r_wT[wb]], writes=[rq])
                flush_pend()
                for _ in range(3):
                    if wbq:
                        wbq.pop(0)()
                if oc == 0 and lnq:
                    ln_tile(lnq.pop(0))
                if oc < 4:
                    rb_ = tt % 2

                    def ev(base, pq=pq):
                        return bass.AP(pq.t, pq.base + base, [[1024, 128], [64, 8], [2, 32]])

                    def evr(base, rb_=rb_):
                        return bass.AP(ropetm, rb_ * 512 + base, [[2 * 512, 128], [64, 8], [2, 32]])

                    def cosap(tt=tt):
                        return bass.AP(cosb, tt * 32, [[ntt * 32, 128], [0, 8], [1, 32]])

                    def sinap(tt=tt):
                        return bass.AP(sinb, tt * 32, [[ntt * 32, 128], [0, 8], [1, 32]])

                    def scrap(k):
                        return bass.AP(scr, k * 256, [[D, 128], [32, 8], [1, 32]])

                    E("V", lambda e, ev=ev, cosap=cosap, scrap=scrap: e.tensor_mul(scrap(0), ev(0), cosap()),
                      reads=[rq, r_cos], writes=[r_scr])
                    E("V", lambda e, ev=ev, sinap=sinap, scrap=scrap: e.tensor_mul(scrap(1), ev(1), sinap()),
                      reads=[rq, r_sin], writes=[r_scr])
                    E("V", lambda e, ev=ev, sinap=sinap, scrap=scrap: e.tensor_mul(scrap(2), ev(0), sinap()),
                      reads=[rq, r_sin], writes=[r_scr])
                    E("V", lambda e, ev=ev, cosap=cosap, scrap=scrap: e.tensor_mul(scrap(3), ev(1), cosap()),
                      reads=[rq, r_cos], writes=[r_scr])
                    E("V", lambda e, evr=evr, scrap=scrap, rb_=rb_: e.tensor_sub(evr(0), scrap(0), scrap(1)),
                      reads=[r_scr], writes=[r_ropetm[rb_]])
                    E("V", lambda e, evr=evr, scrap=scrap, rb_=rb_: e.tensor_add(evr(1), scrap(2), scrap(3)),
                      reads=[r_scr], writes=[r_ropetm[rb_]])
                    def mk_pend(tt=tt, rb_=rb_, oc=oc):
                        fs = []
                        pb, rb2 = p[6 + (tt % 2)], r_p[6 + (tt % 2)]
                        for j in range(4):
                            def f(j=j, rb_=rb_, pb=pb, rb2=rb2):
                                E("T", lambda e, rb_=rb_, j=j, pb=pb: e.transpose(
                                    pb[:, j * 128:(j + 1) * 128],
                                    ropetm[:, rb_ * 512 + j * 128: rb_ * 512 + (j + 1) * 128], idnb[:]),
                                  reads=[r_ropetm[rb_], r_idnb], writes=[rb2])
                            fs.append(f)
                        dst = q_fm if oc < 2 else k_fm
                        if oc < 2:
                            rdst = [rr for j in range(4) for dj in range(2)
                                    for rr in r_qh[(oc % 2) * 8 + 2 * j + dj][tt // 4: tt // 4 + 1]]
                        else:
                            rdst = [r_kh[(oc % 2) * 8 + 2 * j + dj] for j in range(4) for dj in range(2)]
                        ch0 = (oc % 2) * 4

                        def fev(tt=tt, ch0=ch0, pb=pb, rb2=rb2, dst=dst, rdst=rdst):
                            E("S", lambda e, tt=tt, ch0=ch0, pb=pb, dst=dst: e.copy(
                                bass.AP(dst, ch0 * T + tt * 128, [[NIC * T, 128], [T, 4], [1, 128]]),
                                bass.AP(pb, 0, [[1024, 128], [128, 4], [1, 128]])),
                              reads=[rb2], writes=rdst)
                        fs.append(fev)
                        return fs

                    pend[0] = mk_pend()
                else:
                    h0 = (oc - 4) * 8
                    E("S", lambda e, tt=tt, h0=h0, pq=pq: e.copy(
                        bass.AP(v_ag, tt * (NH * 65) + h0 * 65, [[ntt * NH * 65, 128], [65, 8], [1, 64]]),
                        bass.AP(pq.t, pq.base, [[1024, 128], [64, 8], [1, 64]])),
                      reads=[rq], writes=[r_vag])
            flush_pend()
            for op in wbq:
                op()
            if oc == 0:
                while lnq:
                    ln_tile(lnq.pop(0))
                emit_gmean()

        ctx.barrier()
        E("D", lambda e: e.dma_start(maskSb[:], d_masksq[:]), writes=[r_maskSb])
        E("V", lambda e: e.memset(kaug[64:65, :], 1.0), writes=[r_kaug[0], r_kaug[1]])
        for hp2 in range(2):
            for qt in range(ntt):
                npair = ((qt * 128) // 512 + 1) // 2
                pads = [(1, -3.0e4)]
                if npair == 0 or (npair == 1 and qt % 2 == 1):
                    pads.append((2, -3.0e4))
                if npair == 0 or (npair == 1 and qt % 2 == 0):
                    pads.append((3, 1e-30))
                for slot, val in pads:
                    E("V", lambda e, hp2=hp2, qt=qt, slot=slot, val=val: e.memset(
                        sumch[:, hp2 * 64 + qt * 4 + slot: hp2 * 64 + qt * 4 + slot + 1], val),
                      writes=[r_sumch[hp2][qt]])

        wprep = []

        def wadd(eng, fn, rd, wr):
            wprep.append((eng, fn, rd, wr))

        for os_ in range(8):
            sb = os_ % 2
            wadd("D", lambda e, os_=os_, sb=sb: e.dma_start(
                wstage2[:, sb * D:(sb + 1) * D], d_wout[os_ * 128:(os_ + 1) * 128, :]),
                [], [r_wstage2[sb]])
            wadd("V", lambda e, sb=sb: e.tensor_copy(wcast2[:], wstage2[:, sb * D:(sb + 1) * D]),
                 [r_wstage2[sb]], [r_wcast2])
            for half in range(2):
                pb, rb = p[6 + half], r_p[6 + half]
                for j in range(4):
                    dc = half * 4 + j
                    wadd("T", lambda e, dc=dc, j=j, pb=pb: e.transpose(
                        pb[:, j * 128:(j + 1) * 128], wcast2[:, dc * 128:(dc + 1) * 128], idnb[:]),
                        [r_wcast2, r_idnb], [rb])
                wadd("V", lambda e, os_=os_, half=half, pb=pb: e.tensor_copy(
                    bass.AP(woutT, half * 4 * D + os_ * 128, [[NIC * D, 128], [D, 4], [1, 128]]),
                    bass.AP(pb, 0, [[1024, 128], [128, 4], [1, 128]])),
                    [rb], [r_woutT])
        wi = [0]

        def drip(n):
            for _ in range(n):
                if wi[0] < len(wprep):
                    eng, fn, rd, wr = wprep[wi[0]]
                    wi[0] += 1
                    E(eng, fn, reads=rd, writes=wr)

        tail1 = [None]
        tail2 = [None]

        def flush_tail1():
            if tail1[0] is not None:
                for f in tail1[0]:
                    f()
                tail1[0] = None

        def flush_tail():
            flush_tail1()
            if tail2[0] is not None:
                for f in tail2[0]:
                    f()
                tail2[0] = None

        pend_pv = []

        def _pop_pv():
            kt, fn, bd = pend_pv.pop(0)
            if bd:
                flush_tail()
            fn()

        def drain_pv(min_kt):
            while pend_pv and pend_pv[0][0] >= min_kt:
                _pop_pv()

        def drain_pv_n(n):
            for _ in range(n):
                if pend_pv:
                    _pop_pv()

        def gen_passA(h):
            ch = h // 2
            hb = (h % 2) * 64
            kslot = (h % 2) * nkt * 128
            hp = h % 2
            for kt in range(nkt):
                E("G", lambda e, kt=kt, ch=ch, hb=hb, kslot=kslot: e.tensor_copy(
                    kaug[0:64, kslot + kt * 128: kslot + (kt + 1) * 128],
                    k_fm[hb:hb + 64, ch * T + kt * 128: ch * T + (kt + 1) * 128]),
                  reads=[r_kh[h]], writes=[r_kaug[h % 2]])
            for qc in range(nqc):
                sl = (h % 2) * 4 + qc
                E("V", lambda e, qc=qc, ch=ch, hb=hb, sl=sl: e.tensor_copy(
                    qaug[0:64, sl * 512:(sl + 1) * 512],
                    q_fm[hb:hb + 64, ch * T + qc * 512: ch * T + (qc + 1) * 512]),
                  reads=[r_qh[h][qc]], writes=[r_qaug[sl]])
            yield
            for gi in range(ntt):
                qt = gi
                nfull = (qt * 128) // 512
                rem = qt * 128 - nfull * 512
                k0, kw = nfull * 512, rem + 128
                ps, rs = p[gi % 4], r_p[gi % 4]
                E("T", lambda e, qt=qt, k0=k0, kw=kw, ps=ps, ch=ch, hb=hb: e.matmul(
                    ps[:, 0:kw], q_fm[hb:hb + 64, ch * T + qt * 128: ch * T + (qt + 1) * 128],
                    k_fm[hb:hb + 64, ch * T + k0: ch * T + k0 + kw],
                    start=True, stop=True),
                  reads=[r_qh[h][qt // 4], r_kh[h]], writes=[rs])
                E("V", lambda e, ps=ps, rem=rem: e.tensor_add(
                    ps[:, rem:rem + 128], ps[:, rem:rem + 128], maskSb[:]),
                  reads=[r_maskSb, rs], writes=[rs])
                E("V", lambda e, ps=ps, kw=kw, qt=qt, hp=hp: e.reduce_max(
                    sumch[:, hp * 64 + qt * 4: hp * 64 + qt * 4 + 1],
                    ps[:, 0:kw], axis=AX.X),
                  reads=[rs], writes=[r_sumch[hp][qt]])
                if gi == 7:
                    flush_tail()
                if gi % 2 == 1:
                    if gi >= 9:
                        drain_pv_n(2)
                    yield
            pairi = 0
            for qt in range(ntt):
                nfull = (qt * 128) // 512
                npair = (nfull + 1) // 2
                fi = 0
                pq = 0
                while fi < nfull:
                    take = [(f * 512, 512) for f in (fi, fi + 1) if f < nfull]
                    fi += len(take)
                    pair = pairi % 2
                    pairi += 1
                    drain_pv_n(1)
                    halves = []
                    for hi, (k0, kw) in enumerate(take):
                        ps, rs = p[2 * pair + hi], r_p[2 * pair + hi]
                        E("T", lambda e, qt=qt, k0=k0, kw=kw, ps=ps, ch=ch, hb=hb: e.matmul(
                            ps[:, 0:kw], q_fm[hb:hb + 64, ch * T + qt * 128: ch * T + (qt + 1) * 128],
                            k_fm[hb:hb + 64, ch * T + k0: ch * T + k0 + kw],
                            start=True, stop=True),
                          reads=[r_qh[h][qt // 4], r_kh[h]], writes=[rs])
                        halves.append((ps, rs, kw))
                    w_tot = 1024 if len(halves) == 2 else 512
                    onS = (pq == 1) or (npair == 1 and qt % 2 == 1)
                    pq += 1
                    if onS:
                        E("S", lambda e, w_tot=w_tot, qt=qt, hp=hp, pair=pair: e.activation(
                            dumpW[:, 0:w_tot], pp[pair][:, 0:w_tot], AF.Exp,
                            scale=1.0 / 64.0,
                            accum_out=sumch[:, hp * 64 + qt * 4 + 3: hp * 64 + qt * 4 + 4]),
                          reads=[halves[0][1], halves[-1][1]],
                          writes=[r_dumpW[0], r_sumch[hp][qt]])
                    else:
                        E("V", lambda e, w_tot=w_tot, qt=qt, hp=hp, pair=pair: e.reduce_max(
                            sumch[:, hp * 64 + qt * 4 + 2: hp * 64 + qt * 4 + 3],
                            pp[pair][:, 0:w_tot], axis=AX.X),
                          reads=[halves[0][1], halves[-1][1]],
                          writes=[r_sumch[hp][qt]])
                    yield
            drain_pv(-1)
            rall = [r_sumch[hp][qt] for qt in range(ntt)]

            def sc_ap(slot):
                return bass.AP(sumch, hp * 64 + slot, [[128, 128], [4, ntt]])

            cq = cqb
            E("V", lambda e, h=h: e.tensor_max(
                sumq[:, h * ntt:(h + 1) * ntt], sc_ap(0), sc_ap(1)),
              reads=rall, writes=[r_sumq])
            E("V", lambda e, h=h: e.tensor_max(
                sumq[:, h * ntt:(h + 1) * ntt], sumq[:, h * ntt:(h + 1) * ntt], sc_ap(2)),
              reads=rall + [r_sumq], writes=[r_sumq])
            E("S", lambda e, h=h: e.activation(
                cq[:, h * ntt:(h + 1) * ntt], sc_ap(3), AF.Ln),
              reads=rall, writes=[r_cqb[h]])
            E("V", lambda e, h=h: e.scalar_tensor_tensor(
                cq[:, h * ntt:(h + 1) * ntt], cq[:, h * ntt:(h + 1) * ntt], 64.0,
                sumq[:, h * ntt:(h + 1) * ntt], ALU.mult, ALU.max),
              reads=[r_cqb[h], r_sumq], writes=[r_cqb[h]])
            E("V", lambda e, h=h: e.tensor_scalar(
                cq[:, h * ntt:(h + 1) * ntt], cq[:, h * ntt:(h + 1) * ntt],
                -1.0, None, ALU.mult),
              reads=[r_cqb[h]], writes=[r_cqb[h]])
            yield

        def gen_passB(h):
            ch = h // 2
            hb = (h % 2) * 64
            kslot = (h % 2) * nkt * 128
            for qc in range(nqc):
                drip(3)
                nk = 4 * qc + 4
                for j in range(4):
                    col = h * ntt + qc * 4 + j
                    E("T", lambda e, col=col, j=j: e.transpose(
                        pp[2][0:1, 512 + j * 128: 512 + j * 128 + 128], cqb[:, col:col + 1],
                        idn[:]),
                      reads=[r_cqb[h], r_idn], writes=[r_p[5]])
                sl = (h % 2) * 4 + qc
                E("V", lambda e, sl=sl: e.tensor_copy(
                    qaug[64:65, sl * 512:(sl + 1) * 512], pp[2][0:1, 512:1024]),
                  reads=[r_p[5]], writes=[r_qaug[sl]])
                flush_tail1()
                yield
                prs = list(range(nk // 2 - 1, -1, -1))
                for pi, pr in enumerate(prs):
                    drain_pv(2 * pr)
                    pair = pi % 2
                    for hi in range(2):
                        kt = 2 * pr + hi
                        ps, rs = p[2 * pair + hi], r_p[2 * pair + hi]
                        E("T", lambda e, kt=kt, ps=ps, sl=sl, kslot=kslot: e.matmul(
                            ps[:], kaug[:, kslot + kt * 128: kslot + (kt + 1) * 128],
                            qaug[:, sl * 512:(sl + 1) * 512],
                            start=True, stop=True),
                          reads=[r_kaug[h % 2], r_qaug[sl]], writes=[rs])
                    if pi == 1 or (pi == 0 and len(prs) == 1):
                        flush_tail()
                    kt0 = 2 * pr
                    E("S", lambda e, kt0=kt0, pair=pair: e.activation(
                        ptb[:, kt0 * 512:(kt0 + 2) * 512], pp[pair][:, 0:1024], AF.Exp, scale=0.125),
                      reads=[r_p[2 * pair], r_p[2 * pair + 1]],
                      writes=[r_pt[kt0], r_pt[kt0 + 1]])
                    if kt0 + 1 >= 4 * qc:
                        m0 = kt0 - 4 * qc
                        if m0 >= 0:
                            E("G", lambda e, kt0=kt0, m0=m0: e.tensor_mul(
                                ptb[:, kt0 * 512:(kt0 + 2) * 512], ptb[:, kt0 * 512:(kt0 + 2) * 512],
                                maskb[:, m0 * 512:(m0 + 2) * 512]),
                              reads=[r_pt[kt0], r_pt[kt0 + 1], r_maskb],
                              writes=[r_pt[kt0], r_pt[kt0 + 1]])
                        else:
                            E("G", lambda e, kt0=kt0: e.tensor_mul(
                                ptb[:, (kt0 + 1) * 512:(kt0 + 2) * 512],
                                ptb[:, (kt0 + 1) * 512:(kt0 + 2) * 512], maskb[:, 0:512]),
                              reads=[r_pt[kt0 + 1], r_maskb], writes=[r_pt[kt0 + 1]])
                    yield
                drain_pv(-1)
                po, ro = p[4], r_p[4]
                for ki, kt in enumerate(range(nk - 1, -1, -1)):
                    def fpv(e=None, h=h, kt=kt, po=po, ki=ki, nk=nk, ro=ro):
                        E("T", lambda e, h=h, kt=kt, po=po, ki=ki, nk=nk: e.matmul(
                            po[0:65, 0:512],
                            v_ag[:, kt * (NH * 65) + h * 65: kt * (NH * 65) + (h + 1) * 65],
                            ptb[:, kt * 512:(kt + 1) * 512], start=(ki == 0), stop=(ki == nk - 1)),
                          reads=[r_vag, r_pt[kt]], writes=[ro])
                    pend_pv.append((kt, fpv, ki == 0))
                rp = qc % 2

                def mk_tail1(rp=rp, po=po, ro=ro):
                    def t0():
                        E("V", lambda e: e.reciprocal(
                            recips[0:1, rp * 512:(rp + 1) * 512], po[64:65, 0:512]),
                          reads=[ro], writes=[r_recips[rp]])
                        E("D", lambda e: e.dma_start(
                            d_rbc[rp:rp + 1, :], recips[0:1, rp * 512:(rp + 1) * 512]),
                          reads=[r_recips[rp]], writes=[r_rbc_d[rp]])
                    return [t0]

                def mk_tail2(h=h, qc=qc, rp=rp, ch=ch, hb=hb, po=po, ro=ro):
                    def t1():
                        E("D", lambda e: e.dma_start(
                            rbcast[:, rp * 512:(rp + 1) * 512],
                            bass.AP(d_rbc, rp * 512, [[0, 64], [1, 512]])),
                          reads=[r_rbc_d[rp]], writes=[r_rbcast[rp]])

                    def t2():
                        E("S", lambda e: e.copy(
                            otmp[:, rp * 512:(rp + 1) * 512], po[0:64, 0:512]),
                          reads=[ro], writes=[r_otmp[rp]])

                    def t3():
                        E("V", lambda e: e.scalar_tensor_tensor(
                            q_fm[hb:hb + 64, ch * T + qc * 512: ch * T + (qc + 1) * 512],
                            otmp[:, rp * 512:(rp + 1) * 512], 1.0,
                            rbcast[:, rp * 512:(rp + 1) * 512],
                            ALU.mult, ALU.mult),
                          reads=[r_otmp[rp], r_rbcast[rp]], writes=[r_qh[h][qc]])
                    return [t1, t2, t3]

                def fin(mk1=mk_tail1, mk2=mk_tail2):
                    tail1[0] = mk1()
                    tail2[0] = mk2()
                pend_pv.append((-1, fin, True))
                if DRAIN_NOW:
                    drain_pv(-1)
                yield

        for h in range(NH):
            for _ in gen_passA(h):
                pass
            for _ in gen_passB(h):
                pass
        drain_pv(-1)
        flush_tail()
        drip(len(wprep))

        for tc in range(nqc):
            for oc in range(NIC):
                pw, rw = p[oc % 2], r_p[oc % 2]
                for dc in range(NIC):
                    E("T", lambda e, oc=oc, tc=tc, dc=dc, pw=pw: e.matmul(
                        pw[:], woutT[:, dc * D + oc * 128: dc * D + (oc + 1) * 128],
                        q_fm[:, dc * T + tc * 512: dc * T + (tc + 1) * 512],
                        start=(dc == 0), stop=(dc == NIC - 1)),
                      reads=[r_woutT, r_qh[2 * dc][tc], r_qh[2 * dc + 1][tc]], writes=[rw])
                sb = oc % 2
                E("V", lambda e, pw=pw, sb=sb: e.tensor_copy(afst[:, sb * 512:(sb + 1) * 512], pw[:]),
                  reads=[rw], writes=[r_afst[sb]])
                pt_, rt_ = p[2 + sb], r_p[2 + sb]
                for j in range(4):
                    E("T", lambda e, sb=sb, j=j, pt_=pt_: e.transpose(
                        pt_[:, j * 128:(j + 1) * 128],
                        afst[:, sb * 512 + j * 128: sb * 512 + (j + 1) * 128], idn[:]),
                      reads=[r_afst[sb], r_idn], writes=[rt_])
                for j in range(4):
                    E("S", lambda e, oc=oc, j=j, pt_=pt_: e.copy(
                        astage[:, j * D + oc * 128: j * D + (oc + 1) * 128],
                        pt_[:, j * 128:(j + 1) * 128]),
                      reads=[rt_], writes=[r_astage])
            for j in range(4):
                tt = tc * 4 + j
                E("D", lambda e, tt=tt, j=j: e.dma_start(
                    d_attn[tt * 128:(tt + 1) * 128, :], astage[:, j * D:(j + 1) * D]),
                  reads=[r_astage], writes=[r_attnd[tt]])

        ctx.barrier()

        b0q = []

        def q(fn):
            b0q.append(fn)

        def cast_load(dram, rows, cols, dst, rdst, dst_cols, defer=False):
            for icx in range(rows // 128):
                sb = icx % 2

                def f(icx=icx, sb=sb, dram=dram, cols=cols, dst=dst, rdst=rdst,
                      dst_cols=dst_cols):
                    E("D", lambda e, icx=icx, sb=sb, dram=dram, cols=cols: e.dma_start(
                        wstage3[:, sb * D: sb * D + cols], dram[icx * 128:(icx + 1) * 128, :]),
                      writes=[r_wstage3[sb]])
                    E("V", lambda e, icx=icx, sb=sb, dst=dst, cols=cols, dst_cols=dst_cols: e.tensor_copy(
                        dst[:, icx * dst_cols: icx * dst_cols + cols], wstage3[:, sb * D: sb * D + cols]),
                      reads=[r_wstage3[sb]], writes=[rdst])
                if defer:
                    q(f)
                else:
                    f()

        def state0(tt):
            xb = tt % 2
            E("D", lambda e, tt=tt, xb=xb: e.dma_start(
                xload[:, xb * D:(xb + 1) * D], d_x[tt * 128:(tt + 1) * 128, :]),
              writes=[r_xload[xb]])
            E("V", lambda e, tt=tt, xb=xb: e.tensor_scalar(
                state[:, tt * D:(tt + 1) * D], xload[:, xb * D:(xb + 1) * D],
                mcol[:, tt:tt + 1], rstd[:, tt:tt + 1], ALU.subtract, ALU.mult),
              reads=[r_xload[xb], r_mcol[xb], r_rstd[xb]], writes=[r_state[tt]])

        for tt in range(min(4, ntt)):
            state0(tt)
        for tt in range(4, ntt):
            q(lambda tt=tt: state0(tt))
        cast_load(d_cn1, D, D, cn1b, r_cn1b, D)
        cast_load(d_cn2, D, 512, cn2b, r_cn2b, 512, defer=True)
        cast_load(d_gc1, D, 512, gc1b, r_gc1b, 512, defer=True)
        cast_load(d_cn3, 512, 256, cn3b, r_cn3b, 256, defer=True)
        cast_load(d_gc2, 512, 256, gc2b, r_gc2b, 256, defer=True)
        cast_load(d_cn4, 128, 2, cn4b, r_cn4b, 2, defer=True)
        cast_load(d_gc3, 128, 2, gc3b, r_gc3b, 2, defer=True)
        for oc in range(4):
            def f1(oc=oc):
                for ic in range(NIC):
                    E("T", lambda e, oc=oc, ic=ic: e.matmul(
                        p[5][:, oc:oc + 1], gc1b[:, ic * 512 + oc * 128: ic * 512 + (oc + 1) * 128],
                        gmeanb[:, ic:ic + 1], start=(ic == 0), stop=(ic == NIC - 1)),
                      reads=[r_gc1b, r_gmeanb], writes=[r_p[5]])
                E("S", lambda e, oc=oc: e.activation(
                    gb1[:, oc:oc + 1], p[5][:, oc:oc + 1], AF.Gelu, bias=biasb[:, 14 + oc:15 + oc]),
                  reads=[r_p[5], r_biasb], writes=[r_gb1])
            q(f1)
        for oc in range(2):
            def f2(oc=oc):
                for ic in range(4):
                    E("T", lambda e, oc=oc, ic=ic: e.matmul(
                        p[5][:, 8 + oc:9 + oc], gc2b[:, ic * 256 + oc * 128: ic * 256 + (oc + 1) * 128],
                        gb1[:, ic:ic + 1], start=(ic == 0), stop=(ic == 3)),
                      reads=[r_gc2b, r_gb1], writes=[r_p[5]])
                E("S", lambda e, oc=oc: e.activation(
                    gb2[:, oc:oc + 1], p[5][:, 8 + oc:9 + oc], AF.Gelu, bias=biasb[:, 18 + oc:19 + oc]),
                  reads=[r_p[5], r_biasb], writes=[r_gb2])
            q(f2)

        def f3():
            for ic in range(2):
                E("T", lambda e, ic=ic: e.matmul(
                    p[5][0:1, 12:13], gc3b[:, ic:ic + 1], gb2[:, ic:ic + 1],
                    start=(ic == 0), stop=(ic == 1)),
                  reads=[r_gc3b, r_gb2], writes=[r_p[5]])
            E("S", lambda e: e.activation(ab2[0:1, 2:3], p[5][0:1, 12:13], AF.Sigmoid, bias=scal3b[0:1, 1:2]),
              reads=[r_p[5], r_scal3], writes=[r_ab2])
            E("V", lambda e: e.tensor_scalar(ab2[0:1, 3:4], ab2[0:1, 2:3], 1.4, -1.0, ALU.mult, ALU.add),
              reads=[r_ab2], writes=[r_ab2])
            E("V", lambda e: e.tensor_mul(ab2[0:1, 4:5], ab2[0:1, 3:4], scal3b[0:1, 2:3]),
              reads=[r_ab2, r_scal3], writes=[r_ab2])
            E("V", lambda e: e.tensor_scalar(ab2[0:1, 0:1], ab2[0:1, 4:5], 1.0, 0.1, ALU.mult, ALU.add),
              reads=[r_ab2], writes=[r_ab2])
            E("V", lambda e: e.tensor_scalar(ab2[0:1, 1:2], scal3b[0:1, 2:3], 0.6, None, ALU.mult),
              reads=[r_scal3], writes=[r_ab2])
            E("T", lambda e: e.matmul(p[5][0:128, 16:18], ones1[:], ab2[0:1, 0:2], start=True, stop=True),
              reads=[r_ones1, r_ab2], writes=[r_p[5]])
            E("V", lambda e: e.tensor_copy(abb[:], p[5][0:128, 16:18]), reads=[r_p[5]], writes=[r_abb])
        q(f3)

        def drain_b0(n):
            for _ in range(n):
                if b0q:
                    b0q.pop(0)()

        def gen_ffn_A(cyc, tc):
            sp = tc % 2
            for j in range(4):
                tt = tc * 4 + j
                jb = j % 2
                E("V", lambda e, tt=tt, jb=jb: e.tensor_copy(
                    sbtm[:, jb * D:(jb + 1) * D], state[:, tt * D:(tt + 1) * D]),
                  reads=[r_state[tt]], writes=[r_sbtm])
                for half in range(2):
                    pb, rb = p[6 + half], r_p[6 + half]
                    for jj in range(4):
                        ic2 = half * 4 + jj
                        E("T", lambda e, jb=jb, ic2=ic2, jj=jj, pb=pb: e.transpose(
                            pb[:, jj * 128:(jj + 1) * 128],
                            sbtm[:, jb * D + ic2 * 128: jb * D + (ic2 + 1) * 128], idnb[:]),
                          reads=[r_sbtm, r_idnb], writes=[rb])
                    E("V", lambda e, half=half, j=j, pb=pb, sp=sp: e.tensor_copy(
                        bass.AP(sfm, sp * NIC * 512 + half * 4 * 512 + j * 128,
                                [[2 * NIC * 512, 128], [512, 4], [1, 128]]),
                        bass.AP(pb, 0, [[1024, 128], [128, 4], [1, 128]])),
                      reads=[rb], writes=[r_sfm[sp]])
                    yield

        def gen_ffn_B(cyc, tc):
            sp = tc % 2
            if True:
                for oc in range(8):
                    ph, rhh = p[oc % 4], r_p[oc % 4]
                    for ic in range(NIC):
                        E("T", lambda e, oc=oc, ic=ic, ph=ph, sp=sp: e.matmul(
                            ph[:], cn1b[:, ic * D + oc * 128: ic * D + (oc + 1) * 128],
                            sfm[:, sp * NIC * 512 + ic * 512: sp * NIC * 512 + (ic + 1) * 512],
                            start=(ic == 0), stop=(ic == NIC - 1)),
                          reads=[r_cn1b, r_sfm[sp]], writes=[rhh])
                    E("S", lambda e, oc=oc, ph=ph: e.activation(
                        h1b[:, oc * 512:(oc + 1) * 512], ph[:], AF.Gelu, bias=biasb[:, oc:oc + 1]),
                      reads=[rhh, r_biasb], writes=[r_h1])
                    yield
                for oc in range(4):
                    ph, rhh = p[oc % 4], r_p[oc % 4]
                    for ic in range(NIC):
                        E("T", lambda e, oc=oc, ic=ic, ph=ph: e.matmul(
                            ph[:], cn2b[:, ic * 512 + oc * 128: ic * 512 + (oc + 1) * 128],
                            h1b[:, ic * 512:(ic + 1) * 512], start=(ic == 0), stop=(ic == NIC - 1)),
                          reads=[r_cn2b, r_h1], writes=[rhh])
                    E("S", lambda e, oc=oc, ph=ph: e.activation(
                        h2b[:, oc * 512:(oc + 1) * 512], ph[:], AF.Gelu, bias=biasb[:, 8 + oc:9 + oc]),
                      reads=[rhh, r_biasb], writes=[r_h2])
                    yield
                for oc in range(2):
                    ph, rhh = p[oc % 4], r_p[oc % 4]
                    for ic in range(4):
                        E("T", lambda e, oc=oc, ic=ic, ph=ph: e.matmul(
                            ph[:], cn3b[:, ic * 256 + oc * 128: ic * 256 + (oc + 1) * 128],
                            h2b[:, ic * 512:(ic + 1) * 512], start=(ic == 0), stop=(ic == 3)),
                          reads=[r_cn3b, r_h2], writes=[rhh])
                    E("S", lambda e, oc=oc, ph=ph: e.activation(
                        h3b[:, oc * 512:(oc + 1) * 512], ph[:], AF.Gelu, bias=biasb[:, 12 + oc:13 + oc]),
                      reads=[rhh, r_biasb], writes=[r_h3])
                    yield
                for j in range(4):
                    for ic in range(2):
                        E("T", lambda e, j=j, ic=ic: e.matmul(
                            p[4][:, j:j + 1], h3b[:, ic * 512 + j * 128: ic * 512 + (j + 1) * 128],
                            cn4b[:, ic:ic + 1], start=(ic == 0), stop=(ic == 1)),
                          reads=[r_h3, r_cn4b], writes=[r_p[4]])
                drain_b0(999)
                E("V", lambda e, tc=tc: e.tensor_copy(tfb[:, tc * 4: tc * 4 + 4], p[4][:, 0:4]),
                  reads=[r_p[4]], writes=[r_tfb])
                E("S", lambda e, tc=tc: e.activation(
                    tfsg[:, tc * 4: tc * 4 + 4], tfb[:, tc * 4: tc * 4 + 4], AF.Sigmoid,
                    bias=scalB[:, 0:1]),
                  reads=[r_tfb, r_scalB], writes=[r_tfsg])
                E("V", lambda e, tc=tc: e.tensor_scalar(
                    coup[:, tc * 4: tc * 4 + 4], tfsg[:, tc * 4: tc * 4 + 4],
                    abb[:, 1:2], abb[:, 0:1], ALU.mult, ALU.add),
                  reads=[r_tfsg, r_abb], writes=[r_coup])
                E("V", lambda e, tc=tc: e.tensor_scalar(
                    ncoup[:, tc * 4: tc * 4 + 4], coup[:, tc * 4: tc * 4 + 4], -1.0, None, ALU.mult),
                  reads=[r_coup], writes=[r_ncoup])
                yield
                def tw_op(step, b, tt):
                    if step == 0:
                        return lambda: E("V", lambda e, b=b, tt=tt: e.tensor_scalar(
                            tmpq[:, (tt % 2) * 256:(tt % 2) * 256 + 256],
                            state[:, b + 512:b + 768], coup[:, tt:tt + 1], None, ALU.mult),
                          reads=[r_state[tt], r_coup], writes=[r_tmpq[tt % 2]])
                    elif step == 1:
                        return lambda: E("V", lambda e, b=b, tt=tt: e.scalar_tensor_tensor(
                            state[:, b + 512:b + 768], state[:, b + 0:b + 256], ncoup[:, tt:tt + 1],
                            state[:, b + 512:b + 768], ALU.mult, ALU.add),
                          reads=[r_state[tt], r_ncoup], writes=[r_state[tt]])
                    elif step == 2:
                        return lambda: E("V", lambda e, b=b, tt=tt: e.tensor_add(
                            state[:, b + 0:b + 256], state[:, b + 0:b + 256],
                            tmpq[:, (tt % 2) * 256:(tt % 2) * 256 + 256]),
                          reads=[r_tmpq[tt % 2], r_state[tt]], writes=[r_state[tt]])
                    elif step == 3:
                        return lambda: E("V", lambda e, b=b, tt=tt: e.tensor_scalar(
                            tmpq[:, (tt % 2) * 256:(tt % 2) * 256 + 256],
                            state[:, b + 768:b + 1024], coup[:, tt:tt + 1], None, ALU.mult),
                          reads=[r_state[tt], r_coup], writes=[r_tmpq[tt % 2]])
                    elif step == 4:
                        return lambda: E("V", lambda e, b=b, tt=tt: e.scalar_tensor_tensor(
                            state[:, b + 768:b + 1024], state[:, b + 256:b + 512], coup[:, tt:tt + 1],
                            state[:, b + 768:b + 1024], ALU.mult, ALU.add),
                          reads=[r_state[tt], r_coup], writes=[r_state[tt]])
                    else:
                        return lambda: E("V", lambda e, b=b, tt=tt: e.tensor_sub(
                            state[:, b + 256:b + 512], state[:, b + 256:b + 512],
                            tmpq[:, (tt % 2) * 256:(tt % 2) * 256 + 256]),
                          reads=[r_tmpq[tt % 2], r_state[tt]], writes=[r_state[tt]])
                for tp in (tc * 4, tc * 4 + 2):
                    for step in range(6):
                        for tt in (tp, tp + 1):
                            pend_tw.append(tw_op(step, tt * D, tt))

        units = [(c, t) for c in range(3) for t in range(nqc)]
        look = nqc > 1
        pend_tw = []

        def drain_tw(n):
            for _ in range(n):
                if pend_tw:
                    pend_tw.pop(0)()

        for _ in gen_ffn_A(*units[0]):
            drain_b0(2)
        for ui, (c, t) in enumerate(units):
            gb = gen_ffn_B(c, t)
            ga = gen_ffn_A(*units[ui + 1]) if look and ui + 1 < len(units) else None
            if nqc == 2:
                drain_tw(99)
            for _ in gb:
                if ga is not None:
                    next(ga, None)
                drain_tw(2)
                drain_b0(3)
            if ga is not None:
                for _ in ga:
                    pass
            if not look and ui + 1 < len(units):
                drain_tw(99)
                for _ in gen_ffn_A(*units[ui + 1]):
                    pass
        drain_tw(99)

        for tt in range(ntt):
            xb = tt % 2
            E("D", lambda e, tt=tt, xb=xb: e.dma_start(
                xload[:, xb * D:(xb + 1) * D], d_x[tt * 128:(tt + 1) * 128, :]),
              writes=[r_xload[xb]])
            E("D", lambda e, tt=tt, xb=xb: e.dma_start(
                sbtm[:, 0:D].bitcast(F32) if False else wstage3[:, xb * D:(xb + 1) * D],
                d_attn[tt * 128:(tt + 1) * 128, :]),
              reads=[r_attnd[tt]], writes=[r_wstage3[xb]])
            E("V", lambda e, tt=tt, xb=xb: e.tensor_add(
                state[:, tt * D:(tt + 1) * D], state[:, tt * D:(tt + 1) * D],
                xload[:, xb * D:(xb + 1) * D]),
              reads=[r_xload[xb], r_state[tt]], writes=[r_state[tt]])
            E("V", lambda e, tt=tt, xb=xb: e.tensor_add(
                state[:, tt * D:(tt + 1) * D], state[:, tt * D:(tt + 1) * D],
                wstage3[:, xb * D:(xb + 1) * D]),
              reads=[r_wstage3[xb], r_state[tt]], writes=[r_state[tt]])
            E("D", lambda e, tt=tt: e.dma_start(
                d_y[tt * 128:(tt + 1) * 128, :], state[:, tt * D:(tt + 1) * D]),
              reads=[r_state[tt]], writes=[r_y[tt]])

    from contextlib import ExitStack
    with ExitStack() as es:
        for ename in ctx.sem_names():
            ctx.sems[ename] = es.enter_context(nc.semaphore(f"sem_{ename}"))
        block = es.enter_context(nc.Block())

        ctx.begin("dry")
        script()
        final_snap = ctx._snap()

        def run_for(eng):
            def body(proxy):
                ctx.begin("emit", eng, proxy)
                script()
                if eng == "D":
                    for i in range(ctx.NDMA):
                        if final_snap[f"D{i}"] > 0:
                            proxy.wait_ge(ctx.sems[f"D{i}"], final_snap[f"D{i}"])
            return body

        block.tensor(run_for("T"))
        block.vector(run_for("V"))
        block.scalar(run_for("S"))
        block.sync(run_for("D"))
        block.gpsimd(run_for("G"))

    for n in list(cms)[::-1]:
        free(n)
    for cm in pcms[::-1]:
        cm.__exit__(None, None, None)
    return nc


_PROG_CACHE = {}


def _get_prog(T):
    if T not in _PROG_CACHE:
        _PROG_CACHE[T] = build_program(T)
    return _PROG_CACHE[T]


def make_in_maps(T, x_full, W):
    cos_s, sin_s, masks_bf, maskS, idn, idnb = host_constants(T)
    biasp = pack_biases(W["cn_b1"], W["cn_b2"], W["cn_b3"], W["gc_b1"], W["gc_b2"])
    scal3 = np.array([[np.asarray(W["cn_b4"]).reshape(-1)[0],
                       np.asarray(W["gc_b3"]).reshape(-1)[0],
                       np.asarray(W["ar"]).reshape(-1)[0]]], dtype=np.float32)
    cn4p = np.ascontiguousarray(np.asarray(W["cn_w4"], dtype=np.float32).reshape(2, 128).T)
    gc3p = np.ascontiguousarray(np.asarray(W["gc_w3"], dtype=np.float32).reshape(2, 128).T)
    shared = {
        "wqkv": np.ascontiguousarray(np.asarray(W["wqkv"], dtype=np.float32)),
        "wout": np.ascontiguousarray(np.asarray(W["wout"], dtype=np.float32)),
        "cn1": np.ascontiguousarray(np.asarray(W["cn_w1"], dtype=np.float32)),
        "cn2": np.ascontiguousarray(np.asarray(W["cn_w2"], dtype=np.float32)),
        "cn3": np.ascontiguousarray(np.asarray(W["cn_w3"], dtype=np.float32)),
        "cn4p": cn4p,
        "gc1": np.ascontiguousarray(np.asarray(W["gc_w1"], dtype=np.float32)),
        "gc2": np.ascontiguousarray(np.asarray(W["gc_w2"], dtype=np.float32)),
        "gc3p": gc3p,
        "biasp": biasp,
        "scal3": scal3,
        "cosb": cos_s,
        "sinb": sin_s,
        "maskp": masks_bf,
        "masksq": maskS,
        "idn": idn,
        "idnb": idnb,
    }
    maps = []
    for b in range(x_full.shape[0]):
        m = dict(shared)
        m["x"] = np.ascontiguousarray(np.asarray(x_full[b], dtype=np.float32))
        maps.append(m)
    return maps


def kernel(L0_attn_x, L0_attn_W_qkv, L0_attn_W_out,
           L0_ffn_cn_w1, L0_ffn_cn_b1, L0_ffn_cn_w2, L0_ffn_cn_b2,
           L0_ffn_cn_w3, L0_ffn_cn_b3, L0_ffn_cn_w4, L0_ffn_cn_b4,
           L0_ffn_gc_w1, L0_ffn_gc_b1, L0_ffn_gc_w2, L0_ffn_gc_b2,
           L0_ffn_gc_w3, L0_ffn_gc_b3, L0_ffn_adaptive_range):
    x = np.asarray(L0_attn_x, dtype=np.float32)
    B, T, _ = x.shape
    W = {
        "wqkv": L0_attn_W_qkv, "wout": L0_attn_W_out,
        "cn_w1": L0_ffn_cn_w1, "cn_b1": L0_ffn_cn_b1,
        "cn_w2": L0_ffn_cn_w2, "cn_b2": L0_ffn_cn_b2,
        "cn_w3": L0_ffn_cn_w3, "cn_b3": L0_ffn_cn_b3,
        "cn_w4": L0_ffn_cn_w4, "cn_b4": L0_ffn_cn_b4,
        "gc_w1": L0_ffn_gc_w1, "gc_b1": L0_ffn_gc_b1,
        "gc_w2": L0_ffn_gc_w2, "gc_b2": L0_ffn_gc_b2,
        "gc_w3": L0_ffn_gc_w3, "gc_b3": L0_ffn_gc_b3,
        "ar": L0_ffn_adaptive_range,
    }
    nc = _get_prog(T)
    in_maps = make_in_maps(T, x, W)
    res = run_bass_kernel_spmd(nc, in_maps, core_ids=list(range(B)))
    out = np.stack([np.asarray(res.results[b]["y"], dtype=np.float32) for b in range(B)], axis=0)
    return out
